# version 07
# baseline (speedup 1.0000x reference)
import math
import contextlib
import numpy as np
import ml_dtypes
import concourse.bass as bass
import concourse.mybir as mybir
from concourse.bass_utils import run_bass_kernel_spmd

F32 = mybir.dt.float32
BF16 = mybir.dt.bfloat16
AF = mybir.ActivationFunctionType
ALU = mybir.AluOpType
AXX = mybir.AxisListType.X

NCORES = 8
S, D, H, DH = 16384, 4096, 32, 128
TC = 2048
HALO = 256
PAD = TC + 2 * HALO
TILE = 1024
NTILES = TC // TILE
NCH1 = PAD // 128
KT = 20
VSTRIDE = 132
SCALE = 1.0 / math.sqrt(DH)
EPS = 1e-5
NDC = D // 128
TRUNC = None
DBG = None


def _bk(b, off=0, width=512):
    return slice(512 * b + off, 512 * b + off + width)


def build_nc():
    nc = bass.Bass("TRN2", target_bir_lowering=False)

    dx = nc.dram_tensor("xp", [PAD, D], F32, kind="ExternalInput").ap()
    dwq = nc.dram_tensor("wq", [H, 128, NDC, 128], F32, kind="ExternalInput").ap()
    dwo = nc.dram_tensor("wo", [8, 128, NDC, 512], F32, kind="ExternalInput").ap()
    dwb = nc.dram_tensor("wb", [8, 128, NDC, 512], F32, kind="ExternalInput").ap()
    dwk = nc.dram_tensor("wk", [128, NDC, 128], F32, kind="ExternalInput").ap()
    dwv = nc.dram_tensor("wv", [128, NDC, 128], F32, kind="ExternalInput").ap()
    dwar = nc.dram_tensor("war", [128, NDC, 136], F32, kind="ExternalInput").ap()
    dlb = nc.dram_tensor("lb", [128, D], F32, kind="ExternalInput").ap()
    dbb = nc.dram_tensor("bb", [16, D], F32, kind="ExternalInput").ap()
    dcs = nc.dram_tensor("cs", [128, PAD], F32, kind="ExternalInput").ap()
    dsn = nc.dram_tensor("sn", [128, PAD], F32, kind="ExternalInput").ap()
    dal = nc.dram_tensor("al", [128, 16], F32, kind="ExternalInput").ap()
    dam = nc.dram_tensor("am", [128, 16], F32, kind="ExternalInput").ap()
    dvc = nc.dram_tensor("vc", [128, KT], F32, kind="ExternalInput").ap()
    dbr = nc.dram_tensor("br", [128, 8], F32, kind="ExternalInput").ap()
    dkv = nc.dram_tensor("kv", [128, 8], F32, kind="ExternalInput").ap()
    dey = nc.dram_tensor("ey", [128, 128], F32, kind="ExternalInput").ap()
    drs = nc.dram_tensor("rs", [128, 128], BF16, kind="ExternalInput").ap()
    don = nc.dram_tensor("on16", [16, 128], BF16, kind="ExternalInput").ap()
    dout = nc.dram_tensor("out", [TC, D], F32, kind="ExternalOutput").ap()
    ddbg = nc.dram_tensor("dbg", [128, NDC * TILE], BF16, kind="ExternalOutput").ap() if DBG and DBG != "xn" else None
    if DBG == "xn":
        ddbg = nc.dram_tensor("dbg", [128, NCH1 * D], BF16, kind="ExternalOutput").ap()
        ddb2 = nc.dram_tensor("dbg2", [128, NCH1 * D], F32, kind="ExternalOutput").ap()
        GT = NCH1 + (TC // TILE) * (TILE // 128)
        ddb3 = nc.dram_tensor("dbg3", [128, GT * 8], F32, kind="ExternalOutput").ap()

    ctx = contextlib.ExitStack()
    sb = lambda name, shape, dt: ctx.enter_context(nc.sbuf_tensor(name, shape, dt))

    xn_fm = sb("xn_fm", [128, NDC * TILE], BF16)
    qo_fm = sb("qo_fm", [128, NDC * TILE], BF16)
    kf = sb("kf", [128, PAD], BF16)
    vt = sb("vt", [128, KT * VSTRIDE], BF16)
    csq = sb("csq", [128, TILE], F32)
    snq = sb("snq", [128, TILE], F32)
    csk = sb("csk", [128, 256], F32)
    snk = sb("snk", [128, 256], F32)
    wtmp = sb("wtmp", [128, 6144], BF16)
    wob_o = sb("wob_o", [128, 1024], BF16)
    wob_b = sb("wob_b", [128, 1024], BF16)
    xb = sb("xb", [128, 4096], F32)
    xres = sb("xres", [128, 1024], F32)
    outst = sb("outst", [128, 1024], F32)
    sig = sb("sig", [128, 6 * 256], BF16)
    qbf = sb("qbf", [128, 512], BF16)
    rt1 = sb("rt1", [128, 1024], F32)
    rt2 = sb("rt2", [128, 512], F32)
    gpre = sb("gpre", [128, 128], F32)
    g_fm = sb("g_fm", [128, 1024], BF16)
    sx = sb("sx", [128, 512], F32)
    ln_full = sb("ln", [128, 256], F32)
    eye = sb("eye", [128, 128], F32)
    rsw = sb("rsw", [128, 128], BF16)
    ones16 = sb("ones16", [16, 128], BF16)
    bbt = sb("bbt", [16, 512], BF16)
    al_t = sb("al_t", [128, 16], F32)
    am_t = sb("am_t", [128, 16], F32)
    vc_t = sb("vc_t", [128, KT], F32)
    br_t = sb("br_t", [128, 8], F32)
    kv_t = sb("kv_t", [128, 8], F32)
    G_TOTAL = NCH1 + NTILES * (TILE // 128)
    lnx = sb("lnx", [128, G_TOTAL * 8], F32)
    flr = sb("flr", [128, 8], F32)
    ps = ctx.enter_context(nc.psum_tensor("ps", [128, 4096], F32))

    semkeys = ("dc", "ck0", "ck1", "cq", "x0", "x1", "xr0", "xr1", "so0", "so1",
               "wi", "wq0", "wq1", "wq2", "wl", "bb", "wo0", "wo1", "wb0", "wb1",
               "t", "v", "s", "gc")
    sem = {k: ctx.enter_context(nc.semaphore("sem_" + k)) for k in semkeys}
    CNT = {k: 0 for k in sem}
    LASTW = {}
    ENG = {"sy": nc.sync, "gp": nc.gpsimd, "te": nc.tensor, "ve": nc.vector, "sc": nc.scalar}

    DMAK = ("dc", "ck0", "ck1", "cq", "x0", "x1", "xr0", "xr1", "so0", "so1",
            "wi", "wq0", "wq1", "wq2", "wl", "bb", "wo0", "wo1", "wb0", "wb1")
    def inc(instr, key, n=None):
        n = n if n is not None else (16 if key in DMAK else 1)
        CNT[key] += n
        instr.then_inc(sem[key], n)
        return CNT[key]

    def wait(ek, key, val):
        if val <= 0:
            return
        if LASTW.get((ek, key), 0) >= val:
            return
        ENG[ek].wait_ge(sem[key], val)
        LASTW[(ek, key)] = val

    GUARD = {}

    def need(ek, res):
        g = GUARD.get(res)
        if g:
            wait(ek, g[0], g[1])

    def done(res, key, cnt):
        GUARD[res] = (key, cnt)

    def VF():
        return nc.vector.tensor_tensor(out=flr[:, 0:1], in0=flr[:, 4:5],
                                       in1=flr[:, 4:5], op=ALU.add)

    def te_barrier():
        wait("te", "s", CNT["s"])
        wait("te", "v", CNT["v"])

    LNDUMP = [0]
    LNGEN = [0]
    for k in semkeys:
        nc.gpsimd.sem_clear(sem[k])
    nc.all_engine_barrier()

    inc(nc.sync.dma_start(out=eye[:], in_=dey), "dc")
    inc(nc.sync.dma_start(out=rsw[:], in_=drs), "dc")
    inc(nc.sync.dma_start(out=al_t[:], in_=dal), "dc")
    inc(nc.sync.dma_start(out=am_t[:], in_=dam), "dc")
    inc(nc.sync.dma_start(out=vc_t[:], in_=dvc), "dc")
    inc(nc.sync.dma_start(out=br_t[:], in_=dbr), "dc")
    c_consts = inc(nc.sync.dma_start(out=kv_t[:], in_=dkv), "dc")
    WV0 = 24576
    g_wk = inc(nc.gpsimd.dma_start(out=wtmp[:, 0:4096], in_=dwk.rearrange("p c k -> p (c k)")), "wi")
    g_wv = inc(nc.gpsimd.dma_start(out=qo_fm[:, WV0:WV0 + 4096],
                                   in_=dwv.rearrange("p c k -> p (c k)")), "wi")
    c_consts = g_on = inc(nc.sync.dma_start(out=ones16[:], in_=don), "dc")
    nc.vector.memset(flr[:], 0.0)
    nc.vector.memset(qbf[:], 0.0)
    nc.vector.memset(sig[:], 0.0)
    nc.vector.memset(rt1[:], 0.0)
    inc(nc.vector.memset(rt2[:], 0.0), "v")

    def ln_chunk():
        LNGEN[0] += 1
        gen = (LNGEN[0] % 4) * 64
        L = lambda i: ln_full[:, gen + 4 * i: gen + 4 * i + 1]
        g0, g1 = GUARD["xload0"], GUARD["xload1"]
        need("ve", "lnbuf")
        for j in range(2):
            wait("ve", (g0 if j == 0 else g1)[0], (g0 if j == 0 else g1)[1])
            nc.vector.tensor_reduce(out=L(j), in_=xb[:, j * 2048:(j + 1) * 2048],
                                    axis=AXX, op=ALU.add)
            for q in range(4):
                qq = j * 2048 + q * 512
                nc.vector.tensor_tensor(out=rt1[:, 0:512], in0=xb[:, qq:qq + 512],
                                        in1=xb[:, qq:qq + 512], op=ALU.mult)
                nc.vector.tensor_reduce(out=L(2 + j * 4 + q), in_=rt1[:, 0:512], axis=AXX, op=ALU.add)
        nc.vector.tensor_tensor(out=L(10), in0=L(2), in1=L(3), op=ALU.add)
        nc.vector.tensor_tensor(out=L(11), in0=L(4), in1=L(5), op=ALU.add)
        nc.vector.tensor_tensor(out=L(12), in0=L(6), in1=L(7), op=ALU.add)
        nc.vector.tensor_tensor(out=L(13), in0=L(8), in1=L(9), op=ALU.add)
        nc.vector.tensor_tensor(out=L(14), in0=L(10), in1=L(11), op=ALU.add)
        nc.vector.tensor_tensor(out=L(15), in0=L(12), in1=L(13), op=ALU.add)
        nc.vector.tensor_tensor(out=L(11), in0=L(0), in1=L(1), op=ALU.add)
        nc.vector.tensor_tensor(out=L(10), in0=L(14), in1=L(15), op=ALU.add)
        nc.vector.tensor_scalar(out=L(4), in0=L(11), scalar1=1.0 / D, scalar2=0.0,
                                op0=ALU.mult, op1=ALU.add)
        nc.vector.tensor_scalar(out=L(5), in0=L(10), scalar1=1.0 / D, scalar2=0.0,
                                op0=ALU.mult, op1=ALU.add)
        nc.vector.tensor_tensor(out=L(6), in0=L(4), in1=L(4), op=ALU.mult)
        VF()
        nc.vector.tensor_tensor(out=L(7), in0=L(5), in1=L(6), op=ALU.subtract)
        VF()
        nc.vector.tensor_scalar(out=L(8), in0=L(7), scalar1=EPS, scalar2=0.0,
                                op0=ALU.add, op1=ALU.add)
        VF()
        gg = LNGEN[0] - 1
        rcp = lnx[:, 8 * gg: 8 * gg + 1]
        rstd = lnx[:, 8 * gg + 4: 8 * gg + 5]
        nc.vector.reciprocal(out=rcp, in_=L(8))
        vR = inc(VF(), "v")
        wait("sc", "v", vR)
        sR = inc(nc.scalar.activation(out=rstd, in_=rcp, func=AF.Sqrt), "s")
        wait("ve", "s", sR)
        nc.vector.tensor_tensor(out=ln_full[:, gen + 56:gen + 57], in0=L(4), in1=rstd, op=ALU.mult)
        nmr = ln_full[:, gen + 52:gen + 53]
        VF()
        nc.vector.tensor_scalar(out=nmr, in0=ln_full[:, gen + 56:gen + 57], scalar1=-1.0,
                                scalar2=0.0, op0=ALU.mult, op1=ALU.add)
        VF()
        nc.vector.tensor_scalar(out=xb[:, 0:2048], in0=xb[:, 0:2048], scalar1=rstd,
                                scalar2=nmr, op0=ALU.mult, op1=ALU.add)
        vN = inc(nc.vector.tensor_scalar(out=xb[:, 2048:4096], in0=xb[:, 2048:4096],
                                         scalar1=rstd, scalar2=nmr,
                                         op0=ALU.mult, op1=ALU.add), "v")
        GUARD["xnorm"] = ("v", vN)


    def transposes(dest_fn):
        vN = GUARD["xnorm"][1]
        wait("te", "dc", c_consts)
        wait("te", "v", vN)
        last_s = last_v = 0
        for d in range(NDC):
            b = d % 2
            slot = (d // 2) % 4
            need("te", f"trb{b}")
            tT = inc(nc.tensor.transpose(out=ps[:, _bk(b, 128 * slot, 128)],
                                         in_=xb[:, d * 128:(d + 1) * 128], identity=eye[:]), "t")
            if d % 2 == 0:
                wait("sc", "t", tT)
                c = inc(nc.scalar.copy(out=dest_fn(d), in_=ps[:, _bk(b, 128 * slot, 128)]), "s")
                done(f"trb{b}", "s", c)
                last_s = c
            else:
                wait("ve", "t", tT)
                c = inc(nc.vector.tensor_copy(out=dest_fn(d), in_=ps[:, _bk(b, 128 * slot, 128)]), "v")
                done(f"trb{b}", "v", c)
                last_v = c
        done("xb0", "t", CNT["t"])
        done("xb1", "t", CNT["t"])
        return last_s, last_v

    def load_x_chunk(row):
        for j in range(2):
            need("sy", f"xb{j}")
            c = inc(nc.sync.dma_start(out=xb[:, j * 2048:(j + 1) * 2048],
                                     in_=dx[row:row + 128, j * 2048:(j + 1) * 2048]), f"x{j}")
            GUARD[f"xload{j}"] = (f"x{j}", c)

    for ch in range(NCH1):
        par = ch % 2
        need("sy", f"csk{par}")
        ck = inc(nc.sync.dma_start(out=csk[:, par * 128:(par + 1) * 128],
                                   in_=dcs[:, ch * 128:(ch + 1) * 128]), f"ck{par}")
        ck = inc(nc.sync.dma_start(out=snk[:, par * 128:(par + 1) * 128],
                                   in_=dsn[:, ch * 128:(ch + 1) * 128]), f"ck{par}")
        load_x_chunk(ch * 128)
        ln_chunk()
        ev_s, ev_v = transposes(lambda d: xn_fm[:, d * 128:(d + 1) * 128])
        if DBG == "xn":
            wait("sy", "s", ev_s)
            wait("sy", "v", ev_v)
            inc(nc.sync.dma_start(out=ddbg[:, ch * D:(ch + 1) * D],
                                  in_=xn_fm[:, 0:D]), "so0")
            cdump = inc(nc.sync.dma_start(out=ddb2[:, ch * D:(ch + 1) * D],
                                          in_=xb[:, 0:D]), "so0")
            GUARD["xb0"] = GUARD["xb1"] = ("so0", cdump)
            for ek in ("ve", "sc", "te"):
                wait(ek, "so0", cdump)
        kb = 2 + par
        need("te", f"bank{kb}")
        wait("te", "s", ev_s)
        wait("te", "v", ev_v)
        wait("te", "wi", g_wv)
        for d in range(NDC):
            mm = nc.tensor.matmul(out=ps[:, _bk(kb, 0, 128)],
                                  lhsT=wtmp[:, d * 128:(d + 1) * 128],
                                  rhs=xn_fm[:, d * 128:(d + 1) * 128],
                                  start=(d == 0), stop=(d == NDC - 1))
        tK = inc(mm, "t")
        need("sc", "qbf")
        wait("sc", "t", tK)
        sK = inc(nc.scalar.copy(out=qbf[:, 0:128], in_=ps[:, _bk(kb, 0, 128)]), "s")
        ksb = 4 + par
        need("te", f"bank{ksb}")
        wait("te", "s", sK)
        tKs = inc(nc.tensor.matmul(out=ps[:, _bk(ksb, 0, 128)], lhsT=rsw[:],
                                   rhs=qbf[:, 0:128], start=True, stop=True), "t")
        done("qbf", "t", tKs)
        vb = 6 + par
        need("te", f"bank{vb}")
        for d in range(NDC):
            mm = nc.tensor.matmul(out=ps[:, _bk(vb, 0, 128)],
                                  lhsT=xn_fm[:, d * 128:(d + 1) * 128],
                                  rhs=qo_fm[:, WV0 + d * 128:WV0 + (d + 1) * 128],
                                  start=(d == 0), stop=(d == NDC - 1))
        tV = inc(mm, "t")
        wait("ve", "t", tKs)
        wait("ve", f"ck{par}", ck)
        nc.vector.tensor_tensor(out=rt1[:, 0:128], in0=ps[:, _bk(kb, 0, 128)],
                                in1=csk[:, par * 128:(par + 1) * 128], op=ALU.mult)
        nc.vector.tensor_tensor(out=rt2[:, 0:128], in0=ps[:, _bk(ksb, 0, 128)],
                                in1=snk[:, par * 128:(par + 1) * 128], op=ALU.mult)
        vR = inc(nc.vector.tensor_tensor(out=kf[:, ch * 128:(ch + 1) * 128],
                                         in0=rt1[:, 0:128], in1=rt2[:, 0:128], op=ALU.add), "v")
        done(f"bank{kb}", "v", vR)
        done(f"bank{ksb}", "v", vR)
        done(f"csk{par}", "v", vR)
        wait("sc", "t", tV)
        sV = inc(nc.scalar.copy(out=vt[:, ch * VSTRIDE:ch * VSTRIDE + 128],
                                in_=ps[:, _bk(vb, 0, 128)]), "s")
        done(f"bank{vb}", "s", sV)
        wait("ve", "dc", c_consts)
        inc(nc.vector.tensor_copy(out=vt[:, ch * VSTRIDE + 128:ch * VSTRIDE + 129],
                                  in_=vc_t[:, ch:ch + 1]), "v")
    te_barrier()
    for r in range(3):
        done(f"wq3_{r}", "t", CNT["t"])
    if DBG == "kv":
        wait("sy", "v", CNT["v"])
        wait("sy", "s", CNT["s"])
        inc(nc.sync.dma_start(out=ddbg[:, 0:PAD], in_=kf[:]), "so0")
        inc(nc.sync.dma_start(out=ddbg[:, 4096:4096 + KT * VSTRIDE], in_=vt[:]), "so0")
        inc(nc.sync.dma_start(out=ddbg[:, 8192:8192 + 4096],
                              in_=xn_fm[:, 0:4096]), "so0")
        wait("sy", "so0", CNT["so0"])

    if DBG == "xn":
        wait("sy", "v", CNT["v"])
        wait("sy", "s", CNT["s"])
        inc(nc.sync.dma_start(out=ddb3, in_=lnx[:]), "so0")
        wait("sy", "so0", CNT["so0"])

    for tau in range(0 if TRUNC == "p1" else NTILES):
        t_rows = HALO + tau * TILE

        for ch8 in range(TILE // 128):
            load_x_chunk(t_rows + ch8 * 128)
            ln_chunk()
            ev_s, ev_v = transposes(
                lambda d, _c=ch8: xn_fm[:, d * TILE + _c * 128: d * TILE + (_c + 1) * 128])
        te_barrier()
        wait("te", "s", ev_s)
        wait("te", "v", ev_v)

        need("sy", "csq")
        inc(nc.sync.dma_start(out=csq[:], in_=dcs[:, t_rows:t_rows + TILE]), "cq")
        cq = inc(nc.sync.dma_start(out=snq[:], in_=dsn[:, t_rows:t_rows + TILE]), "cq")

        def wq_load(h, half):
            reg = (2 * h + half) % 3
            need("gp", f"wq3_{reg}")
            c = inc(nc.gpsimd.dma_start(
                out=wtmp[:, reg * 2048:(reg + 1) * 2048],
                in_=dwq[h, :, half * 16:(half + 1) * 16, :].rearrange("p c k -> p (c k)")), f"wq{reg}")
            GUARD[f"wqld_{reg}"] = (f"wq{reg}", c)

        wq_load(0, 0)
        wq_load(0, 1)
        wq_load(1, 0)
        for h in range(H):
            tQ = 0
            for sg in range(2):
                qb = sg + 2 * (h % 2)
                need("te", f"bank{qb}")
                for d in range(NDC):
                    reg = (2 * h + d // 16) % 3
                    if d % 16 == 0:
                        gg = GUARD[f"wqld_{reg}"]
                        wait("te", gg[0], gg[1])
                    mm = nc.tensor.matmul(
                        out=ps[:, _bk(qb)],
                        lhsT=wtmp[:, reg * 2048 + (d % 16) * 128: reg * 2048 + (d % 16 + 1) * 128],
                        rhs=xn_fm[:, d * TILE + sg * 512: d * TILE + (sg + 1) * 512],
                        start=(d == 0), stop=(d == NDC - 1))
                tQ = inc(mm, "t")
                need("sc", "qbf")
                wait("sc", "t", tQ)
                sQ = inc(nc.scalar.copy(out=qbf[:], in_=ps[:, _bk(qb)]), "s")
                sw = 4 + sg
                need("te", f"bank{sw}")
                wait("te", "s", sQ)
                tSw = inc(nc.tensor.matmul(out=ps[:, _bk(sw)], lhsT=rsw[:],
                                           rhs=qbf[:], start=True, stop=True), "t")
                done("qbf", "t", tSw)
                wait("ve", "t", tSw)
                wait("ve", "cq", cq)
                if DBG == "qnorope":
                    vQ = inc(nc.vector.tensor_copy(
                        out=qo_fm[:, h * TILE + sg * 512: h * TILE + (sg + 1) * 512],
                        in_=ps[:, _bk(qb)]), "v")
                else:
                    nc.vector.tensor_tensor(out=rt1[:, 0:512], in0=ps[:, _bk(qb)],
                                            in1=csq[:, sg * 512:(sg + 1) * 512], op=ALU.mult)
                    nc.vector.tensor_tensor(out=rt2[:, 0:512], in0=ps[:, _bk(sw)],
                                            in1=snq[:, sg * 512:(sg + 1) * 512], op=ALU.mult)
                    vQ = inc(nc.vector.tensor_tensor(
                        out=qo_fm[:, h * TILE + sg * 512: h * TILE + (sg + 1) * 512],
                        in0=rt1[:, 0:512], in1=rt2[:, 0:512], op=ALU.add), "v")
                done(f"bank{qb}", "v", vQ)
                done(f"bank{sw}", "v", vQ)
            done(f"wq3_{(2 * h) % 3}", "t", tQ)
            done(f"wq3_{(2 * h + 1) % 3}", "t", tQ)
            if h + 1 < H:
                wq_load(h + 1, 1)
            if h + 2 < H:
                wq_load(h + 2, 0)
        done("csq", "v", CNT["v"])
        te_barrier()
        if DBG in ("qr", "qnorope") and tau == 0:
            wait("sy", "v", CNT["v"])
            inc(nc.sync.dma_start(out=ddbg, in_=qo_fm[:]), "so0")
            wait("sy", "so0", CNT["so0"])
        if TRUNC == "p2b":
            break

        for m in range(4):
            B = 4 * tau + m
            kt0 = 2 * B
            acol = 2 * B
            for h in range(H):
                par = h % 2
                sxo = ((m * H + h) % 4) * 64
                sS_list = []
                for j in range(6):
                    bj = j % 3
                    cj = (j // 3) * 256
                    need("te", f"bank{bj}")
                    need("te", f"sig{j}")
                    tS = inc(nc.tensor.matmul(
                        out=ps[:, _bk(bj, cj, 256)],
                        lhsT=kf[:, (kt0 + j) * 128:(kt0 + j + 1) * 128],
                        rhs=qo_fm[:, h * TILE + m * 256: h * TILE + (m + 1) * 256],
                        start=True, stop=True), "t")
                    wait("sc", "t", tS)
                    sS = inc(nc.scalar.activation(out=sig[:, j * 256:(j + 1) * 256],
                                                  in_=ps[:, _bk(bj, cj, 256)],
                                                  func=AF.Sigmoid, scale=SCALE), "s")
                    sS_list.append(sS)
                    done(f"bank{bj}", "s", sS)
                for qs in range(2):
                    bA = (3 + 2 * par) + qs
                    need("te", f"bank{bA}")
                    mm = None
                    for grp in range(3):
                        for jj in range(2):
                            j = grp * 2 + jj
                            wait("te", "s", sS_list[j])
                            mm = nc.tensor.matmul(
                                out=ps[:, _bk(bA, 132 * grp, 129)],
                                lhsT=sig[:, j * 256 + qs * 128: j * 256 + qs * 128 + 128],
                                rhs=vt[:, (kt0 + j) * VSTRIDE:(kt0 + j) * VSTRIDE + 129],
                                start=(jj == 0), stop=(jj == 1))
                    tA = inc(mm, "t")
                    if qs == 1:
                        for j in range(6):
                            done(f"sig{j}", "t", tA)
                    wait("ve", "t", tA)
                    nc.vector.tensor_copy(out=rt1[:, 0:129], in_=ps[:, _bk(bA, 132, 129)])
                    nc.vector.tensor_tensor(out=rt1[:, 132:261], in0=ps[:, _bk(bA, 0, 129)],
                                            in1=rt1[:, 0:129], op=ALU.add)
                    vA = inc(nc.vector.tensor_tensor(out=rt1[:, 264:393], in0=ps[:, _bk(bA, 264, 129)],
                                                     in1=rt1[:, 0:129], op=ALU.add), "v")
                    done(f"bank{bA}", "v", vA)
                    nc.vector.reciprocal(out=sx[:, sxo + 32:sxo + 33], in_=rt1[:, 260:261])
                    nc.vector.reciprocal(out=sx[:, sxo + 33:sxo + 34], in_=rt1[:, 392:393])
                    nc.vector.tensor_tensor(out=sx[:, sxo + 34:sxo + 35], in0=sx[:, sxo + 32:sxo + 33],
                                            in1=am_t[:, acol + qs:acol + qs + 1], op=ALU.mult)
                    nc.vector.tensor_tensor(out=sx[:, sxo + 35:sxo + 36], in0=sx[:, sxo + 33:sxo + 34],
                                            in1=al_t[:, acol + qs:acol + qs + 1], op=ALU.mult)
                    nc.vector.tensor_scalar(out=rt2[:, 0:128], in0=rt1[:, 132:260],
                                            scalar1=sx[:, sxo + 34:sxo + 35],
                                            scalar2=0.0, op0=ALU.mult, op1=ALU.add)
                    need("ve", f"otm{qs}")
                    vO = inc(nc.vector.scalar_tensor_tensor(
                        out=rt2[:, 128 + qs * 128:256 + qs * 128], in0=rt1[:, 264:392],
                        scalar=sx[:, sxo + 35:sxo + 36], in1=rt2[:, 0:128], op0=ALU.mult, op1=ALU.add), "v")
                    need("te", "bank7")
                    wait("te", "v", vO)
                    tT = inc(nc.tensor.transpose(out=ps[:, _bk(7, 256 * qs, 128)],
                                                 in_=rt2[:, 128 + qs * 128:256 + qs * 128],
                                                 identity=eye[:]), "t")
                    done(f"otm{qs}", "t", tT)
                    wait("sc", "t", tT)
                    sO = inc(nc.scalar.copy(
                        out=qo_fm[:, h * TILE + m * 256 + qs * 128:
                                  h * TILE + m * 256 + (qs + 1) * 128],
                        in_=ps[:, _bk(7, 256 * qs, 128)]), "s")
                    done("bank7", "s", sO)
        te_barrier()

        for r in range(3):
            gg = GUARD.get(f"wq3_{r}")
            if gg:
                wait("gp", gg[0], gg[1])
        g_war = inc(nc.gpsimd.dma_start(out=wtmp[:, 0:NDC * 136],
                                        in_=dwar.rearrange("p c k -> p (c k)")), "wl")
        g_fm_ready = 0
        for tt in range(TILE // 128):
            hb = tt % 2
            sxo = (tt % 4) * 64
            need("te", f"bank{hb}")
            wait("te", "wl", g_war)
            for d in range(NDC):
                mm = nc.tensor.matmul(out=ps[:, _bk(hb, 0, 136)],
                                      lhsT=xn_fm[:, d * TILE + tt * 128: d * TILE + (tt + 1) * 128],
                                      rhs=wtmp[:, d * 136:(d + 1) * 136],
                                      start=(d == 0), stop=(d == NDC - 1))
            tH = inc(mm, "t")
            wait("ve", "t", tH)
            wait("ve", "dc", c_consts)
            sbase = 256 + (tt % 2) * 128
            SC8 = lambda o: sx[:, sbase + o: sbase + o + 8]
            SC1 = lambda o: sx[:, sbase + o: sbase + o + 1]
            z0 = SC8(0); zs = SC8(8); zc = SC8(16)
            nc.vector.tensor_tensor(out=z0, in0=ps[:, _bk(hb, 128, 8)], in1=br_t[:], op=ALU.add)
            VF()
            nc.vector.tensor_copy(out=zs, in_=z0)
            VF()
            ces = [(0, 1), (2, 3), (4, 5), (6, 7), (0, 2), (1, 3), (4, 6), (5, 7),
                   (1, 2), (5, 6), (0, 4), (1, 5), (2, 6), (3, 7), (2, 4), (3, 5),
                   (1, 2), (3, 4), (5, 6)]
            for (i, j) in ces:
                nc.vector.tensor_tensor(out=SC1(84), in0=zs[:, i:i + 1],
                                        in1=zs[:, j:j + 1], op=ALU.max)
                nc.vector.tensor_tensor(out=zs[:, j:j + 1], in0=zs[:, i:i + 1],
                                        in1=zs[:, j:j + 1], op=ALU.min)
                nc.vector.tensor_copy(out=zs[:, i:i + 1], in_=SC1(84))
            nc.vector.tensor_tensor(out=SC1(72), in0=zs[:, 0:1], in1=zs[:, 1:2], op=ALU.add)
            nc.vector.tensor_tensor(out=SC1(73), in0=zs[:, 2:3], in1=zs[:, 3:4], op=ALU.add)
            nc.vector.tensor_tensor(out=SC1(74), in0=zs[:, 4:5], in1=zs[:, 5:6], op=ALU.add)
            nc.vector.tensor_tensor(out=SC1(75), in0=zs[:, 6:7], in1=zs[:, 7:8], op=ALU.add)
            nc.vector.tensor_copy(out=zc[:, 0:1], in_=zs[:, 0:1])
            nc.vector.tensor_tensor(out=SC1(76), in0=SC1(72), in1=SC1(73), op=ALU.add)
            nc.vector.tensor_copy(out=zc[:, 1:2], in_=SC1(72))
            nc.vector.tensor_tensor(out=zc[:, 2:3], in0=SC1(72), in1=zs[:, 2:3], op=ALU.add)
            nc.vector.tensor_copy(out=zc[:, 3:4], in_=SC1(76))
            nc.vector.tensor_tensor(out=SC1(77), in0=SC1(76), in1=SC1(74), op=ALU.add)
            nc.vector.tensor_tensor(out=zc[:, 4:5], in0=SC1(76), in1=zs[:, 4:5], op=ALU.add)
            nc.vector.tensor_copy(out=zc[:, 5:6], in_=SC1(77))
            nc.vector.tensor_tensor(out=zc[:, 6:7], in0=SC1(77), in1=zs[:, 6:7], op=ALU.add)
            nc.vector.tensor_tensor(out=zc[:, 7:8], in0=SC1(77), in1=SC1(75), op=ALU.add)
            nc.vector.tensor_tensor(out=SC8(24), in0=zs, in1=kv_t[:], op=ALU.mult)
            VF()
            nc.vector.tensor_scalar(out=SC8(32), in0=SC8(24), scalar1=1.0, scalar2=0.0,
                                    op0=ALU.add, op1=ALU.add)
            VF()
            sup = SC8(40)
            nc.vector.tensor_tensor(out=sup, in0=SC8(32), in1=zc, op=ALU.is_gt)
            VF()
            nc.vector.tensor_reduce(out=SC1(80), in_=sup, axis=AXX, op=ALU.add)
            dsel = SC8(48)
            nc.vector.tensor_copy(out=dsel[:, 7:8], in_=sup[:, 7:8])
            nc.vector.tensor_tensor(out=dsel[:, 0:7], in0=sup[:, 0:7], in1=sup[:, 1:8],
                                    op=ALU.subtract)
            VF()
            nc.vector.tensor_tensor(out=SC8(56), in0=zc, in1=dsel, op=ALU.mult)
            nc.vector.reciprocal(out=SC1(83), in_=SC1(80))
            nc.vector.tensor_reduce(out=SC1(81), in_=SC8(56), axis=AXX, op=ALU.add)
            VF()
            nc.vector.tensor_scalar(out=SC1(82), in0=SC1(81), scalar1=1.0, scalar2=0.0,
                                    op0=ALU.subtract, op1=ALU.add)
            VF()
            nc.vector.tensor_tensor(out=SC1(85), in0=SC1(82), in1=SC1(83), op=ALU.mult)
            VF()
            w8 = SC8(64)
            nc.vector.tensor_scalar(out=w8, in0=z0, scalar1=SC1(85), scalar2=0.0,
                                    op0=ALU.subtract, op1=ALU.max)
            VF()
            need("ve", "gpre")
            for e in range(8):
                mm = nc.vector.tensor_scalar(out=gpre[:, e * 16:(e + 1) * 16],
                                             in0=ps[:, _bk(hb, e * 16, 16)],
                                             scalar1=w8[:, e:e + 1], scalar2=0.0,
                                             op0=ALU.mult, op1=ALU.add)
            vG = inc(VF(), "v")
            done(f"bank{hb}", "v", vG)
            gb = 2 + tt % 2
            need("te", f"bank{gb}")
            wait("te", "v", vG)
            tG = inc(nc.tensor.transpose(out=ps[:, _bk(gb, 0, 128)], in_=gpre[:],
                                         identity=eye[:]), "t")
            done("gpre", "t", tG)
            need("sc", "g_fm")
            wait("sc", "t", tG)
            sG = inc(nc.scalar.copy(out=g_fm[:, tt * 128:(tt + 1) * 128],
                                    in_=ps[:, _bk(gb, 0, 128)]), "s")
            done(f"bank{gb}", "s", sG)
            g_fm_ready = sG
            GUARD["war_used"] = ("t", tH)
        te_barrier()

        need("gp", "war_used")
        g_lb = inc(nc.gpsimd.dma_start(out=wtmp[:, 0:4096], in_=dlb), "wl")
        rows0 = tau * TILE
        for oc in range(8):
            need("gp", "bbt")
            g_bb = inc(nc.gpsimd.dma_start(out=bbt[:], in_=dbb[:, oc * 512:(oc + 1) * 512]), "bb")

            def wob_load(d):
                reg = d % 2
                for nm, buf, dr in (("o", wob_o, dwo), ("b", wob_b, dwb)):
                    need("gp", f"wob{nm}{reg}")
                    c = inc(nc.gpsimd.dma_start(
                        out=buf[:, reg * 512:(reg + 1) * 512],
                        in_=dr[oc, :, d, :]), f"w{nm}{reg}")
                    GUARD[f"wobld{nm}{reg}"] = (f"w{nm}{reg}", c)

            wob_load(0)
            wob_load(1)
            for tt in range(2):
                need("sy", f"xres{tt}")
                c = inc(nc.sync.dma_start(
                    out=xres[:, tt * 512:(tt + 1) * 512],
                    in_=dx[HALO + rows0 + tt * 128: HALO + rows0 + (tt + 1) * 128,
                           oc * 512:(oc + 1) * 512]), f"xr{tt}")
                GUARD[f"xresld{tt}"] = (f"xr{tt}", c)
            for d in range(NDC):
                reg = d % 2
                for nm in ("o", "b"):
                    gg = GUARD[f"wobld{nm}{reg}"]
                    wait("te", gg[0], gg[1])
                mm = None
                for tt in range(8):
                    if d == 0:
                        need("te", f"bank{tt}")
                    nc.tensor.matmul(
                        out=ps[:, _bk(tt)],
                        lhsT=qo_fm[:, d * TILE + tt * 128: d * TILE + (tt + 1) * 128],
                        rhs=wob_o[:, reg * 512:(reg + 1) * 512],
                        start=(d == 0), stop=False)
                for tt in range(8):
                    mm = nc.tensor.matmul(
                        out=ps[:, _bk(tt)],
                        lhsT=xn_fm[:, d * TILE + tt * 128: d * TILE + (tt + 1) * 128],
                        rhs=wob_b[:, reg * 512:(reg + 1) * 512],
                        start=False, stop=False)
                tC = inc(mm, "t")
                done(f"wobo{reg}", "t", tC)
                done(f"wobb{reg}", "t", tC)
                if d + 2 < NDC:
                    wob_load(d + 2)
            wait("te", "wl", g_lb)
            wait("te", "bb", g_bb)
            wait("te", "s", g_fm_ready)
            wait("te", "dc", g_on)
            for tt in range(8):
                nc.tensor.matmul(out=ps[:, _bk(tt)],
                                 lhsT=g_fm[:, tt * 128:(tt + 1) * 128],
                                 rhs=wtmp[:, oc * 512:(oc + 1) * 512],
                                 start=False, stop=False)
            tlast = {}
            for tt in range(8):
                mm = nc.tensor.matmul(out=ps[:, _bk(tt)], lhsT=ones16[:],
                                      rhs=bbt[:], start=False, stop=True)
                tlast[tt] = inc(mm, "t")
            done("bbt", "t", tlast[7])
            for tt in range(8):
                r2 = tt % 2
                wait("ve", "t", tlast[tt])
                gg = GUARD[f"xresld{r2}"]
                wait("ve", gg[0], gg[1])
                need("ve", f"outst{r2}")
                vE = inc(nc.vector.tensor_tensor(out=outst[:, r2 * 512:(r2 + 1) * 512],
                                                 in0=ps[:, _bk(tt)],
                                                 in1=xres[:, r2 * 512:(r2 + 1) * 512],
                                                 op=ALU.add), "v")
                done(f"bank{tt}", "v", vE)
                done(f"xres{r2}", "v", vE)
                wait("sy", "v", vE)
                cS = inc(nc.sync.dma_start(
                    out=dout[rows0 + tt * 128: rows0 + (tt + 1) * 128, oc * 512:(oc + 1) * 512],
                    in_=outst[:, r2 * 512:(r2 + 1) * 512]), f"so{r2}")
                done(f"outst{r2}", f"so{r2}", cS)
                if tt + 2 < 8:
                    need("sy", f"xres{r2}")
                    c = inc(nc.sync.dma_start(
                        out=xres[:, r2 * 512:(r2 + 1) * 512],
                        in_=dx[HALO + rows0 + (tt + 2) * 128: HALO + rows0 + (tt + 3) * 128,
                               oc * 512:(oc + 1) * 512]), f"xr{r2}")
                    GUARD[f"xresld{r2}"] = (f"xr{r2}", c)
        te_barrier()
        t_all = CNT["t"]
        for r in range(3):
            done(f"wq3_{r}", "t", t_all)
        done("wtmp_lb", "t", t_all)
        done("g_fm", "t", t_all)
        done("qo_free", "t", t_all)

    nc.sync.wait_ge(sem["so0"], CNT["so0"])
    nc.sync.wait_ge(sem["so1"], CNT["so1"])
    fin_t = inc(nc.tensor.nop(), "t")
    fin_v = inc(nc.vector.nop(), "v")
    fin_s = inc(nc.scalar.nop(), "s")
    fin_q = inc(nc.sync.nop(), "s", 1)
    nc.gpsimd.wait_ge(sem["t"], fin_t)
    nc.gpsimd.wait_ge(sem["v"], fin_v)
    nc.gpsimd.wait_ge(sem["s"], fin_q)
    for k in DMAK:
        if CNT[k]:
            nc.gpsimd.wait_ge(sem[k], CNT[k])
    for k in semkeys:
        nc.gpsimd.sem_clear(sem[k])
    ctx.close()
    return nc



def _host_prep(inputs):
    x = np.asarray(inputs["L0_attn_x"], dtype=np.float32).reshape(S, D)
    Wq = np.asarray(inputs["L0_attn_Wq"], dtype=np.float32)
    Wk = np.asarray(inputs["L0_attn_Wk"], dtype=np.float32)
    Wv = np.asarray(inputs["L0_attn_Wv"], dtype=np.float32)
    Wo = np.asarray(inputs["L0_attn_Wo"], dtype=np.float32)
    Wb = np.asarray(inputs["L0_ffn_W_base"], dtype=np.float32)
    bb = np.asarray(inputs["L0_ffn_b_base"], dtype=np.float32)
    Wr = np.asarray(inputs["L0_ffn_W_router"], dtype=np.float32)
    br = np.asarray(inputs["L0_ffn_b_router"], dtype=np.float32)
    lA = np.asarray(inputs["L0_ffn_lora_A"], dtype=np.float32)
    lB = np.asarray(inputs["L0_ffn_lora_B"], dtype=np.float32)

    wq_arr = np.ascontiguousarray(Wq.reshape(H, DH, NDC, 128).transpose(0, 3, 2, 1))
    wo_arr = np.ascontiguousarray(Wo.reshape(8, 512, NDC, 128).transpose(0, 3, 2, 1))
    wb_arr = np.ascontiguousarray(Wb.reshape(8, 512, NDC, 128).transpose(0, 3, 2, 1))
    wk_arr = np.ascontiguousarray(Wk.reshape(128, NDC, 128).transpose(2, 1, 0))
    wv_arr = np.ascontiguousarray(Wv.reshape(128, NDC, 128).transpose(2, 1, 0))
    A_t = lA.transpose(1, 0, 2).reshape(D, 128)
    war = np.ascontiguousarray(np.concatenate([A_t, Wr.T], axis=1)
                               .reshape(NDC, 128, 136).transpose(1, 0, 2))
    lb2 = np.ascontiguousarray(lB.reshape(128, D))
    bb16 = np.zeros((16, D), np.float32)
    bb16[0] = bb
    br_rep = np.ascontiguousarray(np.broadcast_to(br[None, :], (128, 8)))
    kvec = np.ascontiguousarray(np.broadcast_to(np.arange(1, 9, dtype=np.float32)[None, :],
                                                (128, 8)))
    eye = np.eye(128, dtype=np.float32)
    rsw = np.zeros((128, 128), np.float32)
    for i in range(128):
        rsw[i, (i + 64) % 128] = 1.0
    rsw = rsw.astype(ml_dtypes.bfloat16)
    on16 = np.zeros((16, 128), np.float32)
    on16[0, :] = 1.0
    on16 = on16.astype(ml_dtypes.bfloat16)

    inv_freq = 1.0 / 10000.0 ** (np.arange(0, DH, 2, dtype=np.float32) / DH)
    t = np.arange(S, dtype=np.float32)
    f = np.outer(t, inv_freq)
    cosf = np.zeros((S, DH), dtype=np.float32)
    sinf = np.zeros((S, DH), dtype=np.float32)
    cosf[:, 0::2] = np.cos(f); cosf[:, 1::2] = np.cos(f)
    sinf[:, 0::2] = np.sin(f); sinf[:, 1::2] = np.sin(f)
    cosT = np.ascontiguousarray(cosf.T)
    sinT = sinf.T.copy()
    sinT[0:64, :] *= -1.0

    alpha = np.linspace(0.0, 1.0, 256, dtype=np.float32)

    in_maps = []
    for c in range(NCORES):
        lo = c * TC - HALO
        xp = np.zeros((PAD, D), np.float32)
        s0, s1 = max(lo, 0), min(lo + PAD, S)
        xp[s0 - lo: s1 - lo] = x[s0:s1]
        cs_c = np.zeros((128, PAD), np.float32)
        sn_c = np.zeros((128, PAD), np.float32)
        cs_c[:, s0 - lo: s1 - lo] = cosT[:, s0:s1]
        sn_c[:, s0 - lo: s1 - lo] = sinT[:, s0:s1]
        al_t = np.empty((128, 16), np.float32)
        am_t = np.empty((128, 16), np.float32)
        for b in range(8):
            for qs in range(2):
                col = 2 * b + qs
                a = alpha[qs * 128:(qs + 1) * 128]
                if c == 0 and b == 0:
                    al_t[:, col] = 1.0
                    am_t[:, col] = 0.0
                else:
                    al_t[:, col] = a
                    am_t[:, col] = 1.0 - a
        vc_t = np.zeros((128, KT), np.float32)
        for kt in range(KT):
            g = lo + kt * 128 + np.arange(128)
            vc_t[:, kt] = ((g >= 0) & (g < S)).astype(np.float32)
        in_maps.append({
            "xp": xp, "wq": wq_arr, "wo": wo_arr, "wb": wb_arr, "wk": wk_arr,
            "wv": wv_arr, "war": war, "lb": lb2, "bb": bb16, "cs": cs_c, "sn": sn_c,
            "al": al_t, "am": am_t, "vc": vc_t, "br": br_rep, "kv": kvec,
            "ey": eye, "rs": rsw, "on16": on16,
        })
    return in_maps


def kernel(**inputs):
    in_maps = _host_prep(inputs)
    nc = build_nc()
    res = run_bass_kernel_spmd(nc, in_maps, core_ids=list(range(NCORES)))
    out = np.concatenate([res.results[c]["out"] for c in range(NCORES)], axis=0)
    return np.ascontiguousarray(out.reshape(1, S, D), dtype=np.float32)


# version 12
# speedup vs baseline: 1.1660x; 1.1660x over previous
import math
import contextlib
import numpy as np
import ml_dtypes
import concourse.bass as bass
import concourse.mybir as mybir
from concourse.bass_utils import run_bass_kernel_spmd

F32 = mybir.dt.float32
BF16 = mybir.dt.bfloat16
AF = mybir.ActivationFunctionType
ALU = mybir.AluOpType
AXX = mybir.AxisListType.X

NCORES = 8
S, D, H, DH = 16384, 4096, 32, 128
TC = 2048
HALO = 256
PAD = TC + 2 * HALO
TILE = 1024
NTILES = TC // TILE
NCH1 = PAD // 128
KT = 20
VSTRIDE = 132
SCALE = 1.0 / math.sqrt(DH)
EPS = 1e-5
NDC = D // 128
TRUNC = None
KV_ILV = True
DBG = None


def _bk(b, off=0, width=512):
    return slice(512 * b + off, 512 * b + off + width)


def build_nc():
    nc = bass.Bass("TRN2", target_bir_lowering=False)

    dx = nc.dram_tensor("xp", [PAD, D], F32, kind="ExternalInput").ap()
    dwq = nc.dram_tensor("wq", [H, 128, NDC, 128], F32, kind="ExternalInput").ap()
    dwo = nc.dram_tensor("wo", [8, 128, NDC, 512], F32, kind="ExternalInput").ap()
    dwb = nc.dram_tensor("wb", [8, 128, NDC, 512], F32, kind="ExternalInput").ap()
    dwk = nc.dram_tensor("wk", [128, NDC, 128], F32, kind="ExternalInput").ap()
    dwv = nc.dram_tensor("wv", [128, NDC, 128], F32, kind="ExternalInput").ap()
    dwar = nc.dram_tensor("war", [128, NDC, 136], F32, kind="ExternalInput").ap()
    dlb = nc.dram_tensor("lb", [128, D], F32, kind="ExternalInput").ap()
    dbb = nc.dram_tensor("bb", [16, D], F32, kind="ExternalInput").ap()
    dcs = nc.dram_tensor("cs", [128, PAD], F32, kind="ExternalInput").ap()
    dsn = nc.dram_tensor("sn", [128, PAD], F32, kind="ExternalInput").ap()
    dal = nc.dram_tensor("al", [128, 16], F32, kind="ExternalInput").ap()
    dam = nc.dram_tensor("am", [128, 16], F32, kind="ExternalInput").ap()
    dvc = nc.dram_tensor("vc", [128, KT], F32, kind="ExternalInput").ap()
    dbr = nc.dram_tensor("br", [128, 8], F32, kind="ExternalInput").ap()
    dkv = nc.dram_tensor("kv", [128, 8], F32, kind="ExternalInput").ap()
    dey = nc.dram_tensor("ey", [128, 128], F32, kind="ExternalInput").ap()
    deyb = nc.dram_tensor("eyb", [128, 128], BF16, kind="ExternalInput").ap()
    drs = nc.dram_tensor("rs", [128, 128], BF16, kind="ExternalInput").ap()
    don = nc.dram_tensor("on16", [16, 128], BF16, kind="ExternalInput").ap()
    dout = nc.dram_tensor("out", [TC, D], F32, kind="ExternalOutput").ap()
    ddbg = nc.dram_tensor("dbg", [128, NDC * TILE], BF16, kind="ExternalOutput").ap() if DBG and DBG != "xn" else None
    if DBG == "xn":
        ddbg = nc.dram_tensor("dbg", [128, NCH1 * D], BF16, kind="ExternalOutput").ap()
        ddb2 = nc.dram_tensor("dbg2", [128, NCH1 * D], F32, kind="ExternalOutput").ap()
        GT = NCH1 + (TC // TILE) * (TILE // 128)
        ddb3 = nc.dram_tensor("dbg3", [128, GT * 8], F32, kind="ExternalOutput").ap()

    ctx = contextlib.ExitStack()
    sb = lambda name, shape, dt: ctx.enter_context(nc.sbuf_tensor(name, shape, dt))

    xn_fm = sb("xn_fm", [128, NDC * TILE], BF16)
    qo_fm = sb("qo_fm", [128, NDC * TILE], BF16)
    kf = sb("kf", [128, PAD], BF16)
    vt = sb("vt", [128, KT * VSTRIDE], BF16)

    wtmp = sb("wtmp", [128, 6144], BF16)
    wob_o = sb("wob_o", [128, 1024], BF16)
    wob_b = sb("wob_b", [128, 1024], BF16)
    xb = sb("xb", [128, 4096], F32)
    xb2 = sb("xb2", [128, 4096], F32)
    XBS = [xb, xb2]
    csq = xb[:, 0:1024]
    snq = xb[:, 1024:2048]
    xres = xb[:, 2048:3072]
    outst = xb[:, 3072:4096]
    sig = sb("sig", [128, 2 * 6 * 256], BF16)
    qbf = sb("qbf", [128, 512], BF16)
    rt1 = sb("rt1", [128, 512], F32)
    rt2 = sb("rt2", [128, 512], F32)
    gpre = sb("gpre", [128, 128], F32)
    g_fm = sb("g_fm", [128, 1024], BF16)
    sx = sb("sx", [128, 512], F32)
    csk_sl = lambda par: sx[:, par * 128:(par + 1) * 128]
    snk_sl = lambda par: sx[:, 256 + par * 128:256 + (par + 1) * 128]
    ln_full = sb("ln", [128, 128], F32)
    eye = sb("eye", [128, 128], F32)
    eyeb = sb("eyeb", [128, 128], BF16)
    att = sb("att", [128, 512], F32)
    rsw = sb("rsw", [128, 128], BF16)
    ones16 = sb("ones16", [16, 128], BF16)
    bbt = sb("bbt", [16, 512], BF16)
    al_t = sb("al_t", [128, 16], F32)
    am_t = sb("am_t", [128, 16], F32)
    vc_t = sb("vc_t", [128, KT], F32)
    br_t = sb("br_t", [128, 8], F32)
    kv_t = sb("kv_t", [128, 8], F32)
    G_TOTAL = NCH1 + NTILES * (TILE // 128)
    lnx = sb("lnx", [128, G_TOTAL * 8], F32)
    flr = sb("flr", [128, 8], F32)
    ps = ctx.enter_context(nc.psum_tensor("ps", [128, 4096], F32))

    semkeys = ("dc", "ck0", "ck1", "cq", "x0", "x1", "x2", "x3", "xr0", "xr1", "so0", "so1",
               "wi", "wq0", "wq1", "wq2", "wl", "bb", "wo0", "wo1", "wb0", "wb1",
               "t", "v", "s", "gc")
    sem = {k: ctx.enter_context(nc.semaphore("sem_" + k)) for k in semkeys}
    CNT = {k: 0 for k in sem}
    LASTW = {}
    ENG = {"sy": nc.sync, "gp": nc.gpsimd, "te": nc.tensor, "ve": nc.vector, "sc": nc.scalar}

    DMAK = ("dc", "ck0", "ck1", "cq", "x0", "x1", "x2", "x3", "xr0", "xr1", "so0", "so1",
            "wi", "wq0", "wq1", "wq2", "wl", "bb", "wo0", "wo1", "wb0", "wb1")
    def inc(instr, key, n=None):
        n = n if n is not None else (16 if key in DMAK else 1)
        CNT[key] += n
        instr.then_inc(sem[key], n)
        return CNT[key]

    def wait(ek, key, val):
        if val <= 0:
            return
        if LASTW.get((ek, key), 0) >= val:
            return
        ENG[ek].wait_ge(sem[key], val)
        LASTW[(ek, key)] = val

    GUARD = {}

    def need(ek, res):
        g = GUARD.get(res)
        if g:
            wait(ek, g[0], g[1])

    def done(res, key, cnt):
        GUARD[res] = (key, cnt)

    def VF():
        return nc.vector.tensor_tensor(out=flr[:, 0:1], in0=flr[:, 4:5],
                                       in1=flr[:, 4:5], op=ALU.add)

    def te_barrier():
        wait("te", "s", CNT["s"])
        wait("te", "v", CNT["v"])

    LNDUMP = [0]
    LNGEN = [0]
    for k in semkeys:
        nc.gpsimd.sem_clear(sem[k])
    nc.all_engine_barrier()

    inc(nc.sync.dma_start(out=eye[:], in_=dey), "dc")
    inc(nc.sync.dma_start(out=eyeb[:], in_=deyb), "dc")
    inc(nc.sync.dma_start(out=rsw[:], in_=drs), "dc")
    inc(nc.sync.dma_start(out=al_t[:], in_=dal), "dc")
    inc(nc.sync.dma_start(out=am_t[:], in_=dam), "dc")
    inc(nc.sync.dma_start(out=vc_t[:], in_=dvc), "dc")
    inc(nc.sync.dma_start(out=br_t[:], in_=dbr), "dc")
    c_consts = inc(nc.sync.dma_start(out=kv_t[:], in_=dkv), "dc")
    WV0 = 24576
    g_wk = inc(nc.gpsimd.dma_start(out=wtmp[:, 0:4096], in_=dwk.rearrange("p c k -> p (c k)")), "wi")
    g_wv = inc(nc.gpsimd.dma_start(out=qo_fm[:, WV0:WV0 + 4096],
                                   in_=dwv.rearrange("p c k -> p (c k)")), "wi")
    c_consts = g_on = inc(nc.sync.dma_start(out=ones16[:], in_=don), "dc")
    nc.vector.memset(flr[:], 0.0)
    nc.vector.memset(qbf[:], 0.0)
    nc.vector.memset(sig[:], 0.0)
    nc.vector.memset(rt1[:], 0.0)
    inc(nc.vector.memset(rt2[:], 0.0), "v")

    def ln_chunk(pi):
        buf = XBS[pi]
        """LN the 128-token chunk in buf (two 2048 pieces) in place.
        All stats cells are stride-4 (one SBUF line each) and rotate over 4
        generations to dodge the cross-engine read-line staleness. A cell is
        written at most once per generation before any read."""
        LNGEN[0] += 1
        gen = (LNGEN[0] % 2) * 64
        L = lambda i: ln_full[:, gen + 4 * i: gen + 4 * i + 1]
        need("ve", "lnbuf")
        for j in range(2):
            for qq4 in (2 * j, 2 * j + 1):
                gq = GUARD[f"xload{pi}_{qq4}"]
                wait("ve", gq[0], gq[1])
            nc.vector.tensor_reduce(out=L(j), in_=buf[:, j * 2048:(j + 1) * 2048],
                                    axis=AXX, op=ALU.add)
            for q in range(4):
                qq = j * 2048 + q * 512
                nc.vector.tensor_tensor(out=rt1[:, 0:512], in0=buf[:, qq:qq + 512],
                                        in1=buf[:, qq:qq + 512], op=ALU.mult)
                nc.vector.tensor_reduce(out=L(2 + j * 4 + q), in_=rt1[:, 0:512], axis=AXX, op=ALU.add)
        nc.vector.tensor_tensor(out=L(10), in0=L(2), in1=L(3), op=ALU.add)
        nc.vector.tensor_tensor(out=L(11), in0=L(4), in1=L(5), op=ALU.add)
        nc.vector.tensor_tensor(out=L(12), in0=L(6), in1=L(7), op=ALU.add)
        nc.vector.tensor_tensor(out=L(13), in0=L(8), in1=L(9), op=ALU.add)
        nc.vector.tensor_tensor(out=L(14), in0=L(10), in1=L(11), op=ALU.add)
        nc.vector.tensor_tensor(out=L(15), in0=L(12), in1=L(13), op=ALU.add)
        nc.vector.tensor_tensor(out=L(11), in0=L(0), in1=L(1), op=ALU.add)
        nc.vector.tensor_tensor(out=L(10), in0=L(14), in1=L(15), op=ALU.add)
        nc.vector.tensor_scalar(out=L(4), in0=L(11), scalar1=1.0 / D, scalar2=0.0,
                                op0=ALU.mult, op1=ALU.add)
        nc.vector.tensor_scalar(out=L(5), in0=L(10), scalar1=1.0 / D, scalar2=0.0,
                                op0=ALU.mult, op1=ALU.add)
        nc.vector.tensor_tensor(out=L(6), in0=L(4), in1=L(4), op=ALU.mult)
        VF()
        nc.vector.tensor_tensor(out=L(7), in0=L(5), in1=L(6), op=ALU.subtract)
        VF()
        nc.vector.tensor_scalar(out=L(8), in0=L(7), scalar1=EPS, scalar2=0.0,
                                op0=ALU.add, op1=ALU.add)
        VF()
        gg = LNGEN[0] - 1
        rcp = lnx[:, 8 * gg: 8 * gg + 1]
        rstd = lnx[:, 8 * gg + 4: 8 * gg + 5]
        nc.vector.reciprocal(out=rcp, in_=L(8))
        vR = inc(VF(), "v")
        wait("sc", "v", vR)
        sR = inc(nc.scalar.activation(out=rstd, in_=rcp, func=AF.Sqrt), "s")
        wait("ve", "s", sR)
        nc.vector.tensor_tensor(out=ln_full[:, gen + 56:gen + 57], in0=L(4), in1=rstd, op=ALU.mult)
        nmr = ln_full[:, gen + 52:gen + 53]
        VF()
        nc.vector.tensor_scalar(out=nmr, in0=ln_full[:, gen + 56:gen + 57], scalar1=-1.0,
                                scalar2=0.0, op0=ALU.mult, op1=ALU.add)
        VF()
        nc.vector.tensor_scalar(out=buf[:, 0:2048], in0=buf[:, 0:2048], scalar1=rstd,
                                scalar2=nmr, op0=ALU.mult, op1=ALU.add)
        vN = inc(nc.vector.tensor_scalar(out=buf[:, 2048:4096], in0=buf[:, 2048:4096],
                                         scalar1=rstd, scalar2=nmr,
                                         op0=ALU.mult, op1=ALU.add), "v")
        GUARD["xnorm"] = ("v", vN)


    def transposes(dest4, pi, kv=None):
        buf = XBS[pi]
        vN = GUARD["xnorm"][1]
        wait("te", "dc", c_consts)
        wait("te", "v", vN)
        if kv is not None:
            need("te", f"bank{kv[0]}")
            need("te", f"bank{kv[1]}")
            wait("te", "wi", g_wv)
        last_s = last_v = 0
        mmv = None

        def kvmm(dk):
            nc.tensor.matmul(out=ps[:, _bk(kv[0], 0, 128)],
                             lhsT=wtmp[:, dk * 128:(dk + 1) * 128],
                             rhs=xn_fm[:, dk * 128:(dk + 1) * 128],
                             start=(dk == 0), stop=(dk == NDC - 1))
            return nc.tensor.matmul(out=ps[:, _bk(kv[1], 0, 128)],
                                    lhsT=xn_fm[:, dk * 128:(dk + 1) * 128],
                                    rhs=qo_fm[:, WV0 + dk * 128:WV0 + (dk + 1) * 128],
                                    start=(dk == 0), stop=(dk == NDC - 1))

        NG = NDC // 4
        for db in range(NG):
            b = db % 2
            need("te", f"trb{b}")
            tT = None
            for q in range(4):
                d = db * 4 + q
                tT = nc.tensor.transpose(out=ps[:, _bk(b, q * 128, 128)],
                                         in_=buf[:, d * 128:(d + 1) * 128],
                                         identity=eye[:])
            tT = inc(tT, "t")
            if db % 2 == 0:
                wait("sc", "t", tT)
                c = inc(nc.scalar.copy(out=dest4(db), in_=ps[:, _bk(b, 0, 512)]), "s")
                done(f"trb{b}", "s", c)
                GUARD[f"evgrp{b}"] = ("s", c)
                last_s = c
            else:
                wait("ve", "t", tT)
                c = inc(nc.vector.tensor_copy(out=dest4(db), in_=ps[:, _bk(b, 0, 512)]), "v")
                done(f"trb{b}", "v", c)
                GUARD[f"evgrp{b}"] = ("v", c)
                last_v = c
            if kv is not None and db >= 1:
                g2 = GUARD[f"evgrp{(db - 1) % 2}"]
                wait("te", g2[0], g2[1])
                for dk in range((db - 1) * 4, db * 4):
                    mmv = kvmm(dk)
        if kv is not None:
            g2 = GUARD[f"evgrp{(NG - 1) % 2}"]
            wait("te", g2[0], g2[1])
            for dk in range(NDC - 4, NDC):
                mmv = kvmm(dk)
            GUARD["kvmm"] = ("t", inc(mmv, "t"))
        done(f"xbuf{pi}", "t", CNT["t"])
        return last_s, last_v

    def load_x_chunk(row, pi):
        buf = XBS[pi]
        need("sy", f"xbuf{pi}")
        for j in range(4):
            c = inc(nc.sync.dma_start(out=buf[:, j * 1024:(j + 1) * 1024],
                                     in_=dx[row:row + 128, j * 1024:(j + 1) * 1024]), f"x{j}")
            GUARD[f"xload{pi}_{j}"] = (f"x{j}", c)

    load_x_chunk(0, 0)
    for ch in range(NCH1):
        par = ch % 2
        pi = ch % 2
        need("sy", f"csk{par}")
        ck = inc(nc.sync.dma_start(out=csk_sl(par),
                                   in_=dcs[:, ch * 128:(ch + 1) * 128]), f"ck{par}")
        ck = inc(nc.sync.dma_start(out=snk_sl(par),
                                   in_=dsn[:, ch * 128:(ch + 1) * 128]), f"ck{par}")
        if ch + 1 < NCH1:
            load_x_chunk((ch + 1) * 128, (ch + 1) % 2)
        ln_chunk(pi)
        kb = 2 + par
        vb = 6 + par
        ev_s, ev_v = transposes(lambda db: xn_fm[:, db * 512:(db + 1) * 512], pi,
                                kv=(kb, vb) if KV_ILV else None)
        if not KV_ILV:
            need("te", f"bank{kb}")
            need("te", f"bank{vb}")
            wait("te", "s", ev_s)
            wait("te", "v", ev_v)
            wait("te", "wi", g_wv)
            for d in range(NDC):
                nc.tensor.matmul(out=ps[:, _bk(kb, 0, 128)],
                                 lhsT=wtmp[:, d * 128:(d + 1) * 128],
                                 rhs=xn_fm[:, d * 128:(d + 1) * 128],
                                 start=(d == 0), stop=(d == NDC - 1))
            for d in range(NDC):
                mm = nc.tensor.matmul(out=ps[:, _bk(vb, 0, 128)],
                                      lhsT=xn_fm[:, d * 128:(d + 1) * 128],
                                      rhs=qo_fm[:, WV0 + d * 128:WV0 + (d + 1) * 128],
                                      start=(d == 0), stop=(d == NDC - 1))
            GUARD["kvmm"] = ("t", inc(mm, "t"))
        tKV = GUARD["kvmm"][1]
        if DBG == "xn":
            wait("sy", "s", ev_s)
            wait("sy", "v", ev_v)
            inc(nc.sync.dma_start(out=ddbg[:, ch * D:(ch + 1) * D],
                                  in_=xn_fm[:, 0:D]), "so0")
            cdump = inc(nc.sync.dma_start(out=ddb2[:, ch * D:(ch + 1) * D],
                                          in_=XBS[pi][:, 0:D]), "so0")
            GUARD[f"xbuf{pi}"] = ("so0", cdump)
            for ek in ("ve", "sc", "te"):
                wait(ek, "so0", cdump)
        need("sc", "qbf")
        wait("sc", "t", tKV)
        sK = inc(nc.scalar.copy(out=qbf[:, 0:128], in_=ps[:, _bk(kb, 0, 128)]), "s")
        ksb = 4 + par
        need("te", f"bank{ksb}")
        wait("te", "s", sK)
        tKs = inc(nc.tensor.matmul(out=ps[:, _bk(ksb, 0, 128)], lhsT=rsw[:],
                                   rhs=qbf[:, 0:128], start=True, stop=True), "t")
        done("qbf", "t", tKs)
        tV = tKV
        wait("ve", "t", tKs)
        wait("ve", f"ck{par}", ck)
        nc.vector.tensor_tensor(out=rt1[:, 0:128], in0=ps[:, _bk(kb, 0, 128)],
                                in1=csk_sl(par), op=ALU.mult)
        nc.vector.tensor_tensor(out=rt2[:, 0:128], in0=ps[:, _bk(ksb, 0, 128)],
                                in1=snk_sl(par), op=ALU.mult)
        vR = inc(nc.vector.tensor_tensor(out=kf[:, ch * 128:(ch + 1) * 128],
                                         in0=rt1[:, 0:128], in1=rt2[:, 0:128], op=ALU.add), "v")
        done(f"bank{kb}", "v", vR)
        done(f"bank{ksb}", "v", vR)
        done(f"csk{par}", "v", vR)
        wait("sc", "t", tV)
        sV = inc(nc.scalar.copy(out=vt[:, ch * VSTRIDE:ch * VSTRIDE + 128],
                                in_=ps[:, _bk(vb, 0, 128)]), "s")
        done(f"bank{vb}", "s", sV)
        wait("ve", "dc", c_consts)
        inc(nc.vector.tensor_copy(out=vt[:, ch * VSTRIDE + 128:ch * VSTRIDE + 129],
                                  in_=vc_t[:, ch:ch + 1]), "v")
    te_barrier()
    for r in range(3):
        done(f"wq3_{r}", "t", CNT["t"])
    if DBG == "kv":
        wait("sy", "v", CNT["v"])
        wait("sy", "s", CNT["s"])
        inc(nc.sync.dma_start(out=ddbg[:, 0:PAD], in_=kf[:]), "so0")
        inc(nc.sync.dma_start(out=ddbg[:, 4096:4096 + KT * VSTRIDE], in_=vt[:]), "so0")
        inc(nc.sync.dma_start(out=ddbg[:, 8192:8192 + 4096],
                              in_=xn_fm[:, 0:4096]), "so0")
        wait("sy", "so0", CNT["so0"])

    if DBG == "xn":
        wait("sy", "v", CNT["v"])
        wait("sy", "s", CNT["s"])
        inc(nc.sync.dma_start(out=ddb3, in_=lnx[:]), "so0")
        wait("sy", "so0", CNT["so0"])

    xn3 = xn_fm[:].rearrange("p (d q) -> p d q", d=NDC)
    qo3h = qo_fm[:].rearrange("p (h q) -> p h q", h=H)
    for tau in range(0 if TRUNC == "p1" else NTILES):
        t_rows = HALO + tau * TILE

        wait("sy", "v", CNT["v"])
        wait("sy", "so0", CNT["so0"])
        wait("sy", "so1", CNT["so1"])
        load_x_chunk(t_rows, 0)
        for ch8 in range(TILE // 128):
            pi = ch8 % 2
            if ch8 + 1 < TILE // 128:
                load_x_chunk(t_rows + (ch8 + 1) * 128, (ch8 + 1) % 2)
            ln_chunk(pi)
            ev_s, ev_v = transposes(
                lambda db, _c=ch8: xn3[:, db * 4:(db + 1) * 4, _c * 128:(_c + 1) * 128], pi)
        te_barrier()
        wait("te", "s", ev_s)
        wait("te", "v", ev_v)

        need("sy", "csq")
        need("sy", "xbuf0")
        inc(nc.sync.dma_start(out=csq[:], in_=dcs[:, t_rows:t_rows + TILE]), "cq")
        cq = inc(nc.sync.dma_start(out=snq[:], in_=dsn[:, t_rows:t_rows + TILE]), "cq")

        def wq_load(h, half):
            reg = (2 * h + half) % 3
            need("gp", f"wq3_{reg}")
            c = inc(nc.gpsimd.dma_start(
                out=wtmp[:, reg * 2048:(reg + 1) * 2048],
                in_=dwq[h, :, half * 16:(half + 1) * 16, :].rearrange("p c k -> p (c k)")), f"wq{reg}")
            GUARD[f"wqld_{reg}"] = (f"wq{reg}", c)

        wq_load(0, 0)
        wq_load(0, 1)
        wq_load(1, 0)
        for h in range(H):
            tQ = 0
            for sg in range(2):
                qb = sg + 2 * (h % 2)
                need("te", f"bank{qb}")
                for d in range(NDC):
                    reg = (2 * h + d // 16) % 3
                    if d % 16 == 0:
                        gg = GUARD[f"wqld_{reg}"]
                        wait("te", gg[0], gg[1])
                    mm = nc.tensor.matmul(
                        out=ps[:, _bk(qb)],
                        lhsT=wtmp[:, reg * 2048 + (d % 16) * 128: reg * 2048 + (d % 16 + 1) * 128],
                        rhs=xn_fm[:, d * TILE + sg * 512: d * TILE + (sg + 1) * 512],
                        start=(d == 0), stop=(d == NDC - 1))
                tQ = inc(mm, "t")
                need("sc", "qbf")
                wait("sc", "t", tQ)
                sQ = inc(nc.scalar.copy(out=qbf[:], in_=ps[:, _bk(qb)]), "s")
                sw = 4 + sg
                need("te", f"bank{sw}")
                wait("te", "s", sQ)
                tSw = inc(nc.tensor.matmul(out=ps[:, _bk(sw)], lhsT=rsw[:],
                                           rhs=qbf[:], start=True, stop=True), "t")
                done("qbf", "t", tSw)
                wait("ve", "t", tSw)
                wait("ve", "cq", cq)
                if DBG == "qnorope":
                    vQ = inc(nc.vector.tensor_copy(
                        out=qo_fm[:, h * TILE + sg * 512: h * TILE + (sg + 1) * 512],
                        in_=ps[:, _bk(qb)]), "v")
                else:
                    nc.vector.tensor_tensor(out=rt1[:, 0:512], in0=ps[:, _bk(qb)],
                                            in1=csq[:, sg * 512:(sg + 1) * 512], op=ALU.mult)
                    nc.vector.tensor_tensor(out=rt2[:, 0:512], in0=ps[:, _bk(sw)],
                                            in1=snq[:, sg * 512:(sg + 1) * 512], op=ALU.mult)
                    vQ = inc(nc.vector.tensor_tensor(
                        out=qo_fm[:, h * TILE + sg * 512: h * TILE + (sg + 1) * 512],
                        in0=rt1[:, 0:512], in1=rt2[:, 0:512], op=ALU.add), "v")
                done(f"bank{qb}", "v", vQ)
                done(f"bank{sw}", "v", vQ)
            done(f"wq3_{(2 * h) % 3}", "t", tQ)
            done(f"wq3_{(2 * h + 1) % 3}", "t", tQ)
            if h + 1 < H:
                wq_load(h + 1, 1)
            if h + 2 < H:
                wq_load(h + 2, 0)
        done("csq", "v", CNT["v"])
        te_barrier()
        if DBG in ("qr", "qnorope") and tau == 0:
            wait("sy", "v", CNT["v"])
            inc(nc.sync.dma_start(out=ddbg, in_=qo_fm[:]), "so0")
            wait("sy", "so0", CNT["so0"])
        if TRUNC == "p2b":
            break

        VOS = {}
        for m in range(4):
            B = 4 * tau + m
            kt0 = 2 * B
            acol = 2 * B
            for h in range(H):
                par = h % 2
                sgb = par * 1536
                sxo = ((m * H + h) % 4) * 64
                sS_list = []
                for j in range(6):
                    bj = j % 3
                    cj = (j // 3) * 256
                    need("te", f"bank{bj}")
                    tS = inc(nc.tensor.matmul(
                        out=ps[:, _bk(bj, cj, 256)],
                        lhsT=kf[:, (kt0 + j) * 128:(kt0 + j + 1) * 128],
                        rhs=qo_fm[:, h * TILE + m * 256: h * TILE + (m + 1) * 256],
                        start=True, stop=True), "t")
                    wait("sc", "t", tS)
                    need("sc", f"sig{par}_{j}")
                    sS = inc(nc.scalar.activation(out=sig[:, sgb + j * 256:sgb + (j + 1) * 256],
                                                  in_=ps[:, _bk(bj, cj, 256)],
                                                  func=AF.Sigmoid, scale=SCALE), "s")
                    sS_list.append(sS)
                    done(f"bank{bj}", "s", sS)
                for qs in range(2):
                    bA = (3 + 2 * par) + qs
                    need("te", f"bank{bA}")
                    mm = None
                    for grp in range(3):
                        for jj in range(2):
                            j = grp * 2 + jj
                            wait("te", "s", sS_list[j])
                            mm = nc.tensor.matmul(
                                out=ps[:, _bk(bA, 132 * grp, 129)],
                                lhsT=sig[:, sgb + j * 256 + qs * 128: sgb + j * 256 + qs * 128 + 128],
                                rhs=vt[:, (kt0 + j) * VSTRIDE:(kt0 + j) * VSTRIDE + 129],
                                start=(jj == 0), stop=(jj == 1))
                    tA = inc(mm, "t")
                    if qs == 1:
                        for j in range(6):
                            done(f"sig{par}_{j}", "t", tA)
                    wait("ve", "t", tA)
                    nc.vector.tensor_copy(out=rt1[:, 0:129], in_=ps[:, _bk(bA, 132, 129)])
                    nc.vector.tensor_tensor(out=rt1[:, 132:261], in0=ps[:, _bk(bA, 0, 129)],
                                            in1=rt1[:, 0:129], op=ALU.add)
                    vA = inc(nc.vector.tensor_tensor(out=rt1[:, 264:393], in0=ps[:, _bk(bA, 264, 129)],
                                                     in1=rt1[:, 0:129], op=ALU.add), "v")
                    done(f"bank{bA}", "v", vA)
                    nc.vector.reciprocal(out=sx[:, sxo + 32:sxo + 33], in_=rt1[:, 260:261])
                    nc.vector.reciprocal(out=sx[:, sxo + 33:sxo + 34], in_=rt1[:, 392:393])
                    nc.vector.tensor_tensor(out=sx[:, sxo + 34:sxo + 35], in0=sx[:, sxo + 32:sxo + 33],
                                            in1=am_t[:, acol + qs:acol + qs + 1], op=ALU.mult)
                    nc.vector.tensor_tensor(out=sx[:, sxo + 35:sxo + 36], in0=sx[:, sxo + 33:sxo + 34],
                                            in1=al_t[:, acol + qs:acol + qs + 1], op=ALU.mult)
                    nc.vector.tensor_scalar(out=rt2[:, 0:128], in0=rt1[:, 132:260],
                                            scalar1=sx[:, sxo + 34:sxo + 35],
                                            scalar2=0.0, op0=ALU.mult, op1=ALU.add)
                    slot = par * 2 + qs
                    need("ve", f"attsl{slot}")
                    VOS[slot] = inc(nc.vector.scalar_tensor_tensor(
                        out=att[:, slot * 128:(slot + 1) * 128], in0=rt1[:, 264:392],
                        scalar=sx[:, sxo + 35:sxo + 36], in1=rt2[:, 0:128], op0=ALU.mult, op1=ALU.add), "v")
                if par == 1:
                    need("te", "bank7")
                    tT = None
                    for sl in range(4):
                        wait("te", "v", VOS[sl])
                        tT = nc.tensor.transpose(out=ps[:, _bk(7, sl * 128, 128)],
                                                 in_=att[:, sl * 128:(sl + 1) * 128],
                                                 identity=eye[:])
                    tTg = inc(tT, "t")
                    for sl in range(4):
                        done(f"attsl{sl}", "t", tTg)
                    dst = qo3h[:, h - 1:h + 1, m * 256:(m + 1) * 256]
                    if (h // 2) % 2 == 0:
                        wait("sc", "t", tTg)
                        c = inc(nc.scalar.copy(out=dst, in_=ps[:, _bk(7, 0, 512)]), "s")
                        done("bank7", "s", c)
                    else:
                        wait("ve", "t", tTg)
                        c = inc(nc.vector.tensor_copy(out=dst, in_=ps[:, _bk(7, 0, 512)]), "v")
                        done("bank7", "v", c)
        te_barrier()

        for r in range(3):
            gg = GUARD.get(f"wq3_{r}")
            if gg:
                wait("gp", gg[0], gg[1])
        g_war = inc(nc.gpsimd.dma_start(out=wtmp[:, 0:NDC * 136],
                                        in_=dwar.rearrange("p c k -> p (c k)")), "wl")
        g_fm_ready = 0
        for tt in range(TILE // 128):
            hb = tt % 2
            sxo = (tt % 4) * 64
            need("te", f"bank{hb}")
            wait("te", "wl", g_war)
            for d in range(NDC):
                mm = nc.tensor.matmul(out=ps[:, _bk(hb, 0, 136)],
                                      lhsT=xn_fm[:, d * TILE + tt * 128: d * TILE + (tt + 1) * 128],
                                      rhs=wtmp[:, d * 136:(d + 1) * 136],
                                      start=(d == 0), stop=(d == NDC - 1))
            tH = inc(mm, "t")
            wait("ve", "t", tH)
            wait("ve", "dc", c_consts)
            sbase = 256 + (tt % 2) * 128
            SC8 = lambda o: sx[:, sbase + o: sbase + o + 8]
            SC1 = lambda o: sx[:, sbase + o: sbase + o + 1]
            z0 = SC8(0); zs = SC8(8); zc = SC8(16)
            nc.vector.tensor_tensor(out=z0, in0=ps[:, _bk(hb, 128, 8)], in1=br_t[:], op=ALU.add)
            VF()
            nc.vector.tensor_copy(out=zs, in_=z0)
            VF()
            ces = [(0, 1), (2, 3), (4, 5), (6, 7), (0, 2), (1, 3), (4, 6), (5, 7),
                   (1, 2), (5, 6), (0, 4), (1, 5), (2, 6), (3, 7), (2, 4), (3, 5),
                   (1, 2), (3, 4), (5, 6)]
            for (i, j) in ces:
                nc.vector.tensor_tensor(out=SC1(84), in0=zs[:, i:i + 1],
                                        in1=zs[:, j:j + 1], op=ALU.max)
                nc.vector.tensor_tensor(out=zs[:, j:j + 1], in0=zs[:, i:i + 1],
                                        in1=zs[:, j:j + 1], op=ALU.min)
                nc.vector.tensor_copy(out=zs[:, i:i + 1], in_=SC1(84))
            nc.vector.tensor_tensor(out=SC1(72), in0=zs[:, 0:1], in1=zs[:, 1:2], op=ALU.add)
            nc.vector.tensor_tensor(out=SC1(73), in0=zs[:, 2:3], in1=zs[:, 3:4], op=ALU.add)
            nc.vector.tensor_tensor(out=SC1(74), in0=zs[:, 4:5], in1=zs[:, 5:6], op=ALU.add)
            nc.vector.tensor_tensor(out=SC1(75), in0=zs[:, 6:7], in1=zs[:, 7:8], op=ALU.add)
            nc.vector.tensor_copy(out=zc[:, 0:1], in_=zs[:, 0:1])
            nc.vector.tensor_tensor(out=SC1(76), in0=SC1(72), in1=SC1(73), op=ALU.add)
            nc.vector.tensor_copy(out=zc[:, 1:2], in_=SC1(72))
            nc.vector.tensor_tensor(out=zc[:, 2:3], in0=SC1(72), in1=zs[:, 2:3], op=ALU.add)
            nc.vector.tensor_copy(out=zc[:, 3:4], in_=SC1(76))
            nc.vector.tensor_tensor(out=SC1(77), in0=SC1(76), in1=SC1(74), op=ALU.add)
            nc.vector.tensor_tensor(out=zc[:, 4:5], in0=SC1(76), in1=zs[:, 4:5], op=ALU.add)
            nc.vector.tensor_copy(out=zc[:, 5:6], in_=SC1(77))
            nc.vector.tensor_tensor(out=zc[:, 6:7], in0=SC1(77), in1=zs[:, 6:7], op=ALU.add)
            nc.vector.tensor_tensor(out=zc[:, 7:8], in0=SC1(77), in1=SC1(75), op=ALU.add)
            nc.vector.tensor_tensor(out=SC8(24), in0=zs, in1=kv_t[:], op=ALU.mult)
            VF()
            nc.vector.tensor_scalar(out=SC8(32), in0=SC8(24), scalar1=1.0, scalar2=0.0,
                                    op0=ALU.add, op1=ALU.add)
            VF()
            sup = SC8(40)
            nc.vector.tensor_tensor(out=sup, in0=SC8(32), in1=zc, op=ALU.is_gt)
            VF()
            nc.vector.tensor_reduce(out=SC1(80), in_=sup, axis=AXX, op=ALU.add)
            dsel = SC8(48)
            nc.vector.tensor_copy(out=dsel[:, 7:8], in_=sup[:, 7:8])
            nc.vector.tensor_tensor(out=dsel[:, 0:7], in0=sup[:, 0:7], in1=sup[:, 1:8],
                                    op=ALU.subtract)
            VF()
            nc.vector.tensor_tensor(out=SC8(56), in0=zc, in1=dsel, op=ALU.mult)
            nc.vector.reciprocal(out=SC1(83), in_=SC1(80))
            nc.vector.tensor_reduce(out=SC1(81), in_=SC8(56), axis=AXX, op=ALU.add)
            VF()
            nc.vector.tensor_scalar(out=SC1(82), in0=SC1(81), scalar1=1.0, scalar2=0.0,
                                    op0=ALU.subtract, op1=ALU.add)
            VF()
            nc.vector.tensor_tensor(out=SC1(85), in0=SC1(82), in1=SC1(83), op=ALU.mult)
            VF()
            w8 = SC8(64)
            nc.vector.tensor_scalar(out=w8, in0=z0, scalar1=SC1(85), scalar2=0.0,
                                    op0=ALU.subtract, op1=ALU.max)
            VF()
            need("ve", "gpre")
            for e in range(8):
                mm = nc.vector.tensor_scalar(out=gpre[:, e * 16:(e + 1) * 16],
                                             in0=ps[:, _bk(hb, e * 16, 16)],
                                             scalar1=w8[:, e:e + 1], scalar2=0.0,
                                             op0=ALU.mult, op1=ALU.add)
            vG = inc(VF(), "v")
            done(f"bank{hb}", "v", vG)
            gb = 2 + tt % 2
            need("te", f"bank{gb}")
            wait("te", "v", vG)
            tG = inc(nc.tensor.transpose(out=ps[:, _bk(gb, 0, 128)], in_=gpre[:],
                                         identity=eye[:]), "t")
            done("gpre", "t", tG)
            need("sc", "g_fm")
            wait("sc", "t", tG)
            sG = inc(nc.scalar.copy(out=g_fm[:, tt * 128:(tt + 1) * 128],
                                    in_=ps[:, _bk(gb, 0, 128)]), "s")
            done(f"bank{gb}", "s", sG)
            g_fm_ready = sG
            GUARD["war_used"] = ("t", tH)
        te_barrier()

        need("gp", "war_used")
        g_lb = inc(nc.gpsimd.dma_start(out=wtmp[:, 0:4096], in_=dlb), "wl")
        need("sy", "xbuf0")
        rows0 = tau * TILE
        for oc in range(8):
            need("gp", "bbt")
            g_bb = inc(nc.gpsimd.dma_start(out=bbt[:], in_=dbb[:, oc * 512:(oc + 1) * 512]), "bb")

            def wob_load(d):
                reg = d % 2
                for nm, buf, dr in (("o", wob_o, dwo), ("b", wob_b, dwb)):
                    need("gp", f"wob{nm}{reg}")
                    c = inc(nc.gpsimd.dma_start(
                        out=buf[:, reg * 512:(reg + 1) * 512],
                        in_=dr[oc, :, d, :]), f"w{nm}{reg}")
                    GUARD[f"wobld{nm}{reg}"] = (f"w{nm}{reg}", c)

            wob_load(0)
            wob_load(1)
            for tt in range(2):
                need("sy", f"xres{tt}")
                c = inc(nc.sync.dma_start(
                    out=xres[:, tt * 512:(tt + 1) * 512],
                    in_=dx[HALO + rows0 + tt * 128: HALO + rows0 + (tt + 1) * 128,
                           oc * 512:(oc + 1) * 512]), f"xr{tt}")
                GUARD[f"xresld{tt}"] = (f"xr{tt}", c)
            for d in range(NDC):
                reg = d % 2
                for nm in ("o", "b"):
                    gg = GUARD[f"wobld{nm}{reg}"]
                    wait("te", gg[0], gg[1])
                mm = None
                for tt in range(8):
                    if d == 0:
                        need("te", f"bank{tt}")
                    nc.tensor.matmul(
                        out=ps[:, _bk(tt)],
                        lhsT=qo_fm[:, d * TILE + tt * 128: d * TILE + (tt + 1) * 128],
                        rhs=wob_o[:, reg * 512:(reg + 1) * 512],
                        start=(d == 0), stop=False)
                for tt in range(8):
                    mm = nc.tensor.matmul(
                        out=ps[:, _bk(tt)],
                        lhsT=xn_fm[:, d * TILE + tt * 128: d * TILE + (tt + 1) * 128],
                        rhs=wob_b[:, reg * 512:(reg + 1) * 512],
                        start=False, stop=False)
                tC = inc(mm, "t")
                done(f"wobo{reg}", "t", tC)
                done(f"wobb{reg}", "t", tC)
                if d + 2 < NDC:
                    wob_load(d + 2)
            wait("te", "wl", g_lb)
            wait("te", "bb", g_bb)
            wait("te", "s", g_fm_ready)
            wait("te", "dc", g_on)
            for tt in range(8):
                nc.tensor.matmul(out=ps[:, _bk(tt)],
                                 lhsT=g_fm[:, tt * 128:(tt + 1) * 128],
                                 rhs=wtmp[:, oc * 512:(oc + 1) * 512],
                                 start=False, stop=False)
            tlast = {}
            for tt in range(8):
                mm = nc.tensor.matmul(out=ps[:, _bk(tt)], lhsT=ones16[:],
                                      rhs=bbt[:], start=False, stop=True)
                tlast[tt] = inc(mm, "t")
            done("bbt", "t", tlast[7])
            for tt in range(8):
                r2 = tt % 2
                wait("ve", "t", tlast[tt])
                gg = GUARD[f"xresld{r2}"]
                wait("ve", gg[0], gg[1])
                need("ve", f"outst{r2}")
                vE = inc(nc.vector.tensor_tensor(out=outst[:, r2 * 512:(r2 + 1) * 512],
                                                 in0=ps[:, _bk(tt)],
                                                 in1=xres[:, r2 * 512:(r2 + 1) * 512],
                                                 op=ALU.add), "v")
                done(f"bank{tt}", "v", vE)
                done(f"xres{r2}", "v", vE)
                wait("sy", "v", vE)
                cS = inc(nc.sync.dma_start(
                    out=dout[rows0 + tt * 128: rows0 + (tt + 1) * 128, oc * 512:(oc + 1) * 512],
                    in_=outst[:, r2 * 512:(r2 + 1) * 512]), f"so{r2}")
                done(f"outst{r2}", f"so{r2}", cS)
                if tt + 2 < 8:
                    need("sy", f"xres{r2}")
                    c = inc(nc.sync.dma_start(
                        out=xres[:, r2 * 512:(r2 + 1) * 512],
                        in_=dx[HALO + rows0 + (tt + 2) * 128: HALO + rows0 + (tt + 3) * 128,
                               oc * 512:(oc + 1) * 512]), f"xr{r2}")
                    GUARD[f"xresld{r2}"] = (f"xr{r2}", c)
        te_barrier()
        t_all = CNT["t"]
        for r in range(3):
            done(f"wq3_{r}", "t", t_all)
        done("wtmp_lb", "t", t_all)
        done("g_fm", "t", t_all)
        done("qo_free", "t", t_all)

    nc.sync.wait_ge(sem["so0"], CNT["so0"])
    nc.sync.wait_ge(sem["so1"], CNT["so1"])
    fin_t = inc(nc.tensor.nop(), "t")
    fin_v = inc(nc.vector.nop(), "v")
    fin_s = inc(nc.scalar.nop(), "s")
    fin_q = inc(nc.sync.nop(), "s", 1)
    nc.gpsimd.wait_ge(sem["t"], fin_t)
    nc.gpsimd.wait_ge(sem["v"], fin_v)
    nc.gpsimd.wait_ge(sem["s"], fin_q)
    for k in DMAK:
        if CNT[k]:
            nc.gpsimd.wait_ge(sem[k], CNT[k])
    for k in semkeys:
        nc.gpsimd.sem_clear(sem[k])
    ctx.close()
    return nc



def _host_prep(inputs):
    x = np.asarray(inputs["L0_attn_x"], dtype=np.float32).reshape(S, D)
    Wq = np.asarray(inputs["L0_attn_Wq"], dtype=np.float32)
    Wk = np.asarray(inputs["L0_attn_Wk"], dtype=np.float32)
    Wv = np.asarray(inputs["L0_attn_Wv"], dtype=np.float32)
    Wo = np.asarray(inputs["L0_attn_Wo"], dtype=np.float32)
    Wb = np.asarray(inputs["L0_ffn_W_base"], dtype=np.float32)
    bb = np.asarray(inputs["L0_ffn_b_base"], dtype=np.float32)
    Wr = np.asarray(inputs["L0_ffn_W_router"], dtype=np.float32)
    br = np.asarray(inputs["L0_ffn_b_router"], dtype=np.float32)
    lA = np.asarray(inputs["L0_ffn_lora_A"], dtype=np.float32)
    lB = np.asarray(inputs["L0_ffn_lora_B"], dtype=np.float32)

    wq_arr = np.ascontiguousarray(Wq.reshape(H, DH, NDC, 128).transpose(0, 3, 2, 1))
    wo_arr = np.ascontiguousarray(Wo.reshape(8, 512, NDC, 128).transpose(0, 3, 2, 1))
    wb_arr = np.ascontiguousarray(Wb.reshape(8, 512, NDC, 128).transpose(0, 3, 2, 1))
    wk_arr = np.ascontiguousarray(Wk.reshape(128, NDC, 128).transpose(2, 1, 0))
    wv_arr = np.ascontiguousarray(Wv.reshape(128, NDC, 128).transpose(2, 1, 0))
    A_t = lA.transpose(1, 0, 2).reshape(D, 128)
    war = np.ascontiguousarray(np.concatenate([A_t, Wr.T], axis=1)
                               .reshape(NDC, 128, 136).transpose(1, 0, 2))
    lb2 = np.ascontiguousarray(lB.reshape(128, D))
    bb16 = np.zeros((16, D), np.float32)
    bb16[0] = bb
    br_rep = np.ascontiguousarray(np.broadcast_to(br[None, :], (128, 8)))
    kvec = np.ascontiguousarray(np.broadcast_to(np.arange(1, 9, dtype=np.float32)[None, :],
                                                (128, 8)))
    eye = np.eye(128, dtype=np.float32)
    eyb = np.eye(128, dtype=np.float32).astype(ml_dtypes.bfloat16)
    rsw = np.zeros((128, 128), np.float32)
    for i in range(128):
        rsw[i, (i + 64) % 128] = 1.0
    rsw = rsw.astype(ml_dtypes.bfloat16)
    on16 = np.zeros((16, 128), np.float32)
    on16[0, :] = 1.0
    on16 = on16.astype(ml_dtypes.bfloat16)

    inv_freq = 1.0 / 10000.0 ** (np.arange(0, DH, 2, dtype=np.float32) / DH)
    t = np.arange(S, dtype=np.float32)
    f = np.outer(t, inv_freq)
    cosf = np.zeros((S, DH), dtype=np.float32)
    sinf = np.zeros((S, DH), dtype=np.float32)
    cosf[:, 0::2] = np.cos(f); cosf[:, 1::2] = np.cos(f)
    sinf[:, 0::2] = np.sin(f); sinf[:, 1::2] = np.sin(f)
    cosT = np.ascontiguousarray(cosf.T)
    sinT = sinf.T.copy()
    sinT[0:64, :] *= -1.0

    alpha = np.linspace(0.0, 1.0, 256, dtype=np.float32)

    in_maps = []
    for c in range(NCORES):
        lo = c * TC - HALO
        xp = np.zeros((PAD, D), np.float32)
        s0, s1 = max(lo, 0), min(lo + PAD, S)
        xp[s0 - lo: s1 - lo] = x[s0:s1]
        cs_c = np.zeros((128, PAD), np.float32)
        sn_c = np.zeros((128, PAD), np.float32)
        cs_c[:, s0 - lo: s1 - lo] = cosT[:, s0:s1]
        sn_c[:, s0 - lo: s1 - lo] = sinT[:, s0:s1]
        al_t = np.empty((128, 16), np.float32)
        am_t = np.empty((128, 16), np.float32)
        for b in range(8):
            for qs in range(2):
                col = 2 * b + qs
                a = alpha[qs * 128:(qs + 1) * 128]
                if c == 0 and b == 0:
                    al_t[:, col] = 1.0
                    am_t[:, col] = 0.0
                else:
                    al_t[:, col] = a
                    am_t[:, col] = 1.0 - a
        vc_t = np.zeros((128, KT), np.float32)
        for kt in range(KT):
            g = lo + kt * 128 + np.arange(128)
            vc_t[:, kt] = ((g >= 0) & (g < S)).astype(np.float32)
        in_maps.append({
            "xp": xp, "wq": wq_arr, "wo": wo_arr, "wb": wb_arr, "wk": wk_arr,
            "wv": wv_arr, "war": war, "lb": lb2, "bb": bb16, "cs": cs_c, "sn": sn_c,
            "al": al_t, "am": am_t, "vc": vc_t, "br": br_rep, "kv": kvec,
            "ey": eye, "eyb": eyb, "rs": rsw, "on16": on16,
        })
    return in_maps


def kernel(**inputs):
    in_maps = _host_prep(inputs)
    nc = build_nc()
    res = run_bass_kernel_spmd(nc, in_maps, core_ids=list(range(NCORES)))
    out = np.concatenate([res.results[c]["out"] for c in range(NCORES)], axis=0)
    return np.ascontiguousarray(out.reshape(1, S, D), dtype=np.float32)


# version 15
# speedup vs baseline: 1.2491x; 1.0713x over previous
import math
import contextlib
import numpy as np
import ml_dtypes
import concourse.bass as bass
import concourse.mybir as mybir
from concourse.bass_utils import run_bass_kernel_spmd

F32 = mybir.dt.float32
BF16 = mybir.dt.bfloat16
AF = mybir.ActivationFunctionType
ALU = mybir.AluOpType
AXX = mybir.AxisListType.X

NCORES = 8
S, D, H, DH = 16384, 4096, 32, 128
TC = 2048
HALO = 256
PAD = TC + 2 * HALO
TILE = 1024
NTILES = TC // TILE
NCH1 = PAD // 128
KT = 20
VSTRIDE = 132
SCALE = 1.0 / math.sqrt(DH)
EPS = 1e-5
NDC = D // 128
TRUNC = None
KV_ILV = True
DBG = None


def _bk(b, off=0, width=512):
    return slice(512 * b + off, 512 * b + off + width)


def build_nc():
    nc = bass.Bass("TRN2", target_bir_lowering=False)

    dx = nc.dram_tensor("xp", [PAD, D], F32, kind="ExternalInput").ap()
    dwq = nc.dram_tensor("wq", [H, 128, NDC, 128], F32, kind="ExternalInput").ap()
    dwo = nc.dram_tensor("wo", [8, 128, NDC, 512], F32, kind="ExternalInput").ap()
    dwb = nc.dram_tensor("wb", [8, 128, NDC, 512], F32, kind="ExternalInput").ap()
    dwk = nc.dram_tensor("wk", [128, NDC, 128], F32, kind="ExternalInput").ap()
    dwv = nc.dram_tensor("wv", [128, NDC, 128], F32, kind="ExternalInput").ap()
    dwar = nc.dram_tensor("war", [128, NDC, 136], F32, kind="ExternalInput").ap()
    dlb = nc.dram_tensor("lb", [128, D], F32, kind="ExternalInput").ap()
    dbb = nc.dram_tensor("bb", [16, D], F32, kind="ExternalInput").ap()
    dcs = nc.dram_tensor("cs", [128, PAD], F32, kind="ExternalInput").ap()
    dsn = nc.dram_tensor("sn", [128, PAD], F32, kind="ExternalInput").ap()
    dal = nc.dram_tensor("al", [128, 16], F32, kind="ExternalInput").ap()
    dam = nc.dram_tensor("am", [128, 16], F32, kind="ExternalInput").ap()
    dvc = nc.dram_tensor("vc", [128, KT], F32, kind="ExternalInput").ap()
    dbr = nc.dram_tensor("br", [128, 8], F32, kind="ExternalInput").ap()
    dkv = nc.dram_tensor("kv", [128, 8], F32, kind="ExternalInput").ap()
    dey = nc.dram_tensor("ey", [128, 128], F32, kind="ExternalInput").ap()
    deyb = nc.dram_tensor("eyb", [128, 128], BF16, kind="ExternalInput").ap()
    drs = nc.dram_tensor("rs", [128, 128], BF16, kind="ExternalInput").ap()
    don = nc.dram_tensor("on16", [16, 128], BF16, kind="ExternalInput").ap()
    dout = nc.dram_tensor("out", [TC, D], F32, kind="ExternalOutput").ap()
    ddbg = nc.dram_tensor("dbg", [128, NDC * TILE], BF16, kind="ExternalOutput").ap() if DBG and DBG != "xn" else None
    if DBG == "xn":
        ddbg = nc.dram_tensor("dbg", [128, NCH1 * D], BF16, kind="ExternalOutput").ap()
        ddb2 = nc.dram_tensor("dbg2", [128, NCH1 * D], F32, kind="ExternalOutput").ap()
        GT = NCH1 + (TC // TILE) * (TILE // 128)
        ddb3 = nc.dram_tensor("dbg3", [128, GT * 8], F32, kind="ExternalOutput").ap()

    ctx = contextlib.ExitStack()
    sb = lambda name, shape, dt: ctx.enter_context(nc.sbuf_tensor(name, shape, dt))

    xn_fm = sb("xn_fm", [128, NDC * TILE], BF16)
    qo_fm = sb("qo_fm", [128, NDC * TILE], BF16)
    kf = sb("kf", [128, PAD], BF16)
    vt = sb("vt", [128, KT * VSTRIDE], BF16)

    wtmp = sb("wtmp", [128, 6144], BF16)
    wob_o = sb("wob_o", [128, 1024], BF16)
    wob_b = sb("wob_b", [128, 1024], BF16)
    xb = sb("xb", [128, 4096], F32)
    xb2 = sb("xb2", [128, 4096], F32)
    XBS = [xb, xb2]
    csq = xb[:, 0:1024]
    snq = xb[:, 1024:2048]
    xres = xb[:, 2048:3072]
    outst = xb[:, 3072:4096]
    sig = sb("sig", [128, 2 * 6 * 256], BF16)
    qbf = sb("qbf", [128, 512], BF16)
    rt1 = sb("rt1", [128, 512], F32)
    rt2 = sb("rt2", [128, 512], F32)
    gpre = sb("gpre", [128, 128], F32)
    g_fm = sb("g_fm", [128, 1024], BF16)
    sx = sb("sx", [128, 512], F32)
    csk_sl = lambda par: sx[:, par * 128:(par + 1) * 128]
    snk_sl = lambda par: sx[:, 256 + par * 128:256 + (par + 1) * 128]
    ln_full = sb("ln", [128, 128], F32)
    eye = sb("eye", [128, 128], F32)
    eyeb = sb("eyeb", [128, 128], BF16)
    att = sb("att", [128, 512], F32)
    rsw = sb("rsw", [128, 128], BF16)
    ones16 = sb("ones16", [16, 128], BF16)
    bbt = sb("bbt", [16, 512], BF16)
    al_t = sb("al_t", [128, 16], F32)
    am_t = sb("am_t", [128, 16], F32)
    vc_t = sb("vc_t", [128, KT], F32)
    br_t = sb("br_t", [128, 8], F32)
    kv_t = sb("kv_t", [128, 8], F32)
    G_TOTAL = NCH1 + NTILES * (TILE // 128)
    lnx = sb("lnx", [128, G_TOTAL * 8], F32)
    flr = sb("flr", [128, 8], F32)
    ps = ctx.enter_context(nc.psum_tensor("ps", [128, 4096], F32))

    semkeys = ("dc", "ck0", "ck1", "cq", "x0", "x1", "x2", "x3", "xr0", "xr1", "so0", "so1",
               "wi", "wq0", "wq1", "wq2", "wl", "bb", "wo0", "wo1", "wb0", "wb1",
               "t", "v", "s", "gc")
    sem = {k: ctx.enter_context(nc.semaphore("sem_" + k)) for k in semkeys}
    CNT = {k: 0 for k in sem}
    LASTW = {}
    ENG = {"sy": nc.sync, "gp": nc.gpsimd, "te": nc.tensor, "ve": nc.vector, "sc": nc.scalar}

    DMAK = ("dc", "ck0", "ck1", "cq", "x0", "x1", "x2", "x3", "xr0", "xr1", "so0", "so1",
            "wi", "wq0", "wq1", "wq2", "wl", "bb", "wo0", "wo1", "wb0", "wb1")
    def inc(instr, key, n=None):
        n = n if n is not None else (16 if key in DMAK else 1)
        CNT[key] += n
        instr.then_inc(sem[key], n)
        return CNT[key]

    def wait(ek, key, val):
        if val <= 0:
            return
        if LASTW.get((ek, key), 0) >= val:
            return
        ENG[ek].wait_ge(sem[key], val)
        LASTW[(ek, key)] = val

    GUARD = {}

    def need(ek, res):
        g = GUARD.get(res)
        if g:
            wait(ek, g[0], g[1])

    def done(res, key, cnt):
        GUARD[res] = (key, cnt)

    def VF():
        return nc.vector.tensor_tensor(out=flr[:, 0:1], in0=flr[:, 4:5],
                                       in1=flr[:, 4:5], op=ALU.add)

    def te_barrier():
        wait("te", "s", CNT["s"])
        wait("te", "v", CNT["v"])

    LNDUMP = [0]
    LNGEN = [0]
    for k in semkeys:
        nc.gpsimd.sem_clear(sem[k])
    nc.all_engine_barrier()

    inc(nc.sync.dma_start(out=eye[:], in_=dey), "dc")
    inc(nc.sync.dma_start(out=eyeb[:], in_=deyb), "dc")
    inc(nc.sync.dma_start(out=rsw[:], in_=drs), "dc")
    inc(nc.sync.dma_start(out=al_t[:], in_=dal), "dc")
    inc(nc.sync.dma_start(out=am_t[:], in_=dam), "dc")
    inc(nc.sync.dma_start(out=vc_t[:], in_=dvc), "dc")
    inc(nc.sync.dma_start(out=br_t[:], in_=dbr), "dc")
    c_consts = inc(nc.sync.dma_start(out=kv_t[:], in_=dkv), "dc")
    WV0 = 24576
    g_wk = inc(nc.gpsimd.dma_start(out=wtmp[:, 0:4096], in_=dwk.rearrange("p c k -> p (c k)")), "wi")
    g_wv = inc(nc.gpsimd.dma_start(out=qo_fm[:, WV0:WV0 + 4096],
                                   in_=dwv.rearrange("p c k -> p (c k)")), "wi")
    c_consts = g_on = inc(nc.sync.dma_start(out=ones16[:], in_=don), "dc")
    wait("ve", "dc", c_consts)
    for _kt in range(KT):
        inc(nc.vector.tensor_copy(out=vt[:, _kt * VSTRIDE + 128:_kt * VSTRIDE + 129],
                                  in_=vc_t[:, _kt:_kt + 1]), "v")
    nc.vector.memset(flr[:], 0.0)
    nc.vector.memset(qbf[:], 0.0)
    nc.vector.memset(sig[:], 0.0)
    nc.vector.memset(rt1[:], 0.0)
    inc(nc.vector.memset(rt2[:], 0.0), "v")

    def ln_chunk(pi):
        buf = XBS[pi]
        """LN the 128-token chunk in buf (two 2048 pieces) in place.
        All stats cells are stride-4 (one SBUF line each) and rotate over 4
        generations to dodge the cross-engine read-line staleness. A cell is
        written at most once per generation before any read."""
        LNGEN[0] += 1
        gen = (LNGEN[0] % 2) * 64
        L = lambda i: ln_full[:, gen + 4 * i: gen + 4 * i + 1]
        need("ve", "lnbuf")
        for j in range(2):
            for qq4 in (2 * j, 2 * j + 1):
                gq = GUARD[f"xload{pi}_{qq4}"]
                wait("ve", gq[0], gq[1])
            nc.vector.tensor_reduce(out=L(j), in_=buf[:, j * 2048:(j + 1) * 2048],
                                    axis=AXX, op=ALU.add)
            for q in range(4):
                qq = j * 2048 + q * 512
                nc.vector.tensor_tensor(out=rt1[:, 0:512], in0=buf[:, qq:qq + 512],
                                        in1=buf[:, qq:qq + 512], op=ALU.mult)
                nc.vector.tensor_reduce(out=L(2 + j * 4 + q), in_=rt1[:, 0:512], axis=AXX, op=ALU.add)
        nc.vector.tensor_tensor(out=L(10), in0=L(2), in1=L(3), op=ALU.add)
        nc.vector.tensor_tensor(out=L(11), in0=L(4), in1=L(5), op=ALU.add)
        nc.vector.tensor_tensor(out=L(12), in0=L(6), in1=L(7), op=ALU.add)
        nc.vector.tensor_tensor(out=L(13), in0=L(8), in1=L(9), op=ALU.add)
        nc.vector.tensor_tensor(out=L(14), in0=L(10), in1=L(11), op=ALU.add)
        nc.vector.tensor_tensor(out=L(15), in0=L(12), in1=L(13), op=ALU.add)
        nc.vector.tensor_tensor(out=L(11), in0=L(0), in1=L(1), op=ALU.add)
        nc.vector.tensor_tensor(out=L(10), in0=L(14), in1=L(15), op=ALU.add)
        nc.vector.tensor_scalar(out=L(4), in0=L(11), scalar1=1.0 / D, scalar2=0.0,
                                op0=ALU.mult, op1=ALU.add)
        nc.vector.tensor_scalar(out=L(5), in0=L(10), scalar1=1.0 / D, scalar2=0.0,
                                op0=ALU.mult, op1=ALU.add)
        nc.vector.tensor_tensor(out=L(6), in0=L(4), in1=L(4), op=ALU.mult)
        VF()
        nc.vector.tensor_tensor(out=L(7), in0=L(5), in1=L(6), op=ALU.subtract)
        VF()
        nc.vector.tensor_scalar(out=L(8), in0=L(7), scalar1=EPS, scalar2=0.0,
                                op0=ALU.add, op1=ALU.add)
        VF()
        gg = LNGEN[0] - 1
        rcp = lnx[:, 8 * gg: 8 * gg + 1]
        rstd = lnx[:, 8 * gg + 4: 8 * gg + 5]
        nc.vector.reciprocal(out=rcp, in_=L(8))
        vR = inc(VF(), "v")
        wait("sc", "v", vR)
        sR = inc(nc.scalar.activation(out=rstd, in_=rcp, func=AF.Sqrt), "s")
        wait("ve", "s", sR)
        nc.vector.tensor_tensor(out=ln_full[:, gen + 56:gen + 57], in0=L(4), in1=rstd, op=ALU.mult)
        nmr = ln_full[:, gen + 52:gen + 53]
        VF()
        nc.vector.tensor_scalar(out=nmr, in0=ln_full[:, gen + 56:gen + 57], scalar1=-1.0,
                                scalar2=0.0, op0=ALU.mult, op1=ALU.add)
        VF()
        nc.vector.tensor_scalar(out=buf[:, 0:2048], in0=buf[:, 0:2048], scalar1=rstd,
                                scalar2=nmr, op0=ALU.mult, op1=ALU.add)
        vN = inc(nc.vector.tensor_scalar(out=buf[:, 2048:4096], in0=buf[:, 2048:4096],
                                         scalar1=rstd, scalar2=nmr,
                                         op0=ALU.mult, op1=ALU.add), "v")
        GUARD["xnorm"] = ("v", vN)


    def transposes(dest4, pi, kv=None):
        buf = XBS[pi]
        vN = GUARD["xnorm"][1]
        wait("te", "dc", c_consts)
        wait("te", "v", vN)
        if kv is not None:
            need("te", f"bank{kv[0]}")
            need("te", f"bank{kv[1]}")
            wait("te", "wi", g_wv)
        last_s = last_v = 0
        mmv = None

        def kvmm(dk):
            nc.tensor.matmul(out=ps[:, _bk(kv[0], 0, 128)],
                             lhsT=wtmp[:, dk * 128:(dk + 1) * 128],
                             rhs=xn_fm[:, dk * 128:(dk + 1) * 128],
                             start=(dk == 0), stop=(dk == NDC - 1))
            return nc.tensor.matmul(out=ps[:, _bk(kv[1], 0, 128)],
                                    lhsT=xn_fm[:, dk * 128:(dk + 1) * 128],
                                    rhs=qo_fm[:, WV0 + dk * 128:WV0 + (dk + 1) * 128],
                                    start=(dk == 0), stop=(dk == NDC - 1))

        NG = NDC // 4
        for db in range(NG):
            b = db % 2
            need("te", f"trb{b}")
            tT = None
            for q in range(4):
                d = db * 4 + q
                tT = nc.tensor.transpose(out=ps[:, _bk(b, q * 128, 128)],
                                         in_=buf[:, d * 128:(d + 1) * 128],
                                         identity=eye[:])
            tT = inc(tT, "t")
            wait("sc", "t", tT)
            c = inc(nc.scalar.copy(out=dest4(db), in_=ps[:, _bk(b, 0, 512)]), "s")
            done(f"trb{b}", "s", c)
            GUARD[f"evgrp{b}"] = ("s", c)
            last_s = c
            if kv is not None and db >= 1:
                g2 = GUARD[f"evgrp{(db - 1) % 2}"]
                wait("te", g2[0], g2[1])
                for dk in range((db - 1) * 4, db * 4):
                    mmv = kvmm(dk)
        if kv is not None:
            g2 = GUARD[f"evgrp{(NG - 1) % 2}"]
            wait("te", g2[0], g2[1])
            for dk in range(NDC - 4, NDC):
                mmv = kvmm(dk)
            GUARD["kvmm"] = ("t", inc(mmv, "t"))
        done(f"xbuf{pi}", "t", CNT["t"])
        return last_s, last_v

    def load_x_chunk(row, pi):
        buf = XBS[pi]
        need("sy", f"xbuf{pi}")
        for j in range(4):
            c = inc(nc.sync.dma_start(out=buf[:, j * 1024:(j + 1) * 1024],
                                     in_=dx[row:row + 128, j * 1024:(j + 1) * 1024]), f"x{j}")
            GUARD[f"xload{pi}_{j}"] = (f"x{j}", c)

    PREV_ROPE = [None]

    def emit_rope(par, ch, ck, tKs):
        kb_, ksb_ = 2 + par, 4 + par
        wait("ve", "t", tKs)
        wait("ve", f"ck{par}", ck)
        nc.vector.tensor_tensor(out=rt1[:, 0:128], in0=ps[:, _bk(kb_, 0, 128)],
                                in1=csk_sl(par), op=ALU.mult)
        nc.vector.tensor_tensor(out=rt2[:, 0:128], in0=ps[:, _bk(ksb_, 0, 128)],
                                in1=snk_sl(par), op=ALU.mult)
        vR = inc(nc.vector.tensor_tensor(out=kf[:, ch * 128:(ch + 1) * 128],
                                         in0=rt1[:, 0:128], in1=rt2[:, 0:128], op=ALU.add), "v")
        done(f"bank{kb_}", "v", vR)
        done(f"bank{ksb_}", "v", vR)
        done(f"csk{par}", "v", vR)

    load_x_chunk(0, 0)
    for ch in range(NCH1):
        par = ch % 2
        pi = ch % 2
        need("sy", f"csk{par}")
        ck = inc(nc.sync.dma_start(out=csk_sl(par),
                                   in_=dcs[:, ch * 128:(ch + 1) * 128]), f"ck{par}")
        ck = inc(nc.sync.dma_start(out=snk_sl(par),
                                   in_=dsn[:, ch * 128:(ch + 1) * 128]), f"ck{par}")
        if ch + 1 < NCH1:
            load_x_chunk((ch + 1) * 128, (ch + 1) % 2)
        ln_chunk(pi)
        kb = 2 + par
        vb = 6 + par
        ev_s, ev_v = transposes(lambda db: xn_fm[:, db * 512:(db + 1) * 512], pi,
                                kv=(kb, vb) if KV_ILV else None)
        if not KV_ILV:
            need("te", f"bank{kb}")
            need("te", f"bank{vb}")
            wait("te", "s", ev_s)
            wait("te", "v", ev_v)
            wait("te", "wi", g_wv)
            for d in range(NDC):
                nc.tensor.matmul(out=ps[:, _bk(kb, 0, 128)],
                                 lhsT=wtmp[:, d * 128:(d + 1) * 128],
                                 rhs=xn_fm[:, d * 128:(d + 1) * 128],
                                 start=(d == 0), stop=(d == NDC - 1))
            for d in range(NDC):
                mm = nc.tensor.matmul(out=ps[:, _bk(vb, 0, 128)],
                                      lhsT=xn_fm[:, d * 128:(d + 1) * 128],
                                      rhs=qo_fm[:, WV0 + d * 128:WV0 + (d + 1) * 128],
                                      start=(d == 0), stop=(d == NDC - 1))
            GUARD["kvmm"] = ("t", inc(mm, "t"))
        tKV = GUARD["kvmm"][1]
        if DBG == "xn":
            wait("sy", "s", ev_s)
            wait("sy", "v", ev_v)
            inc(nc.sync.dma_start(out=ddbg[:, ch * D:(ch + 1) * D],
                                  in_=xn_fm[:, 0:D]), "so0")
            cdump = inc(nc.sync.dma_start(out=ddb2[:, ch * D:(ch + 1) * D],
                                          in_=XBS[pi][:, 0:D]), "so0")
            GUARD[f"xbuf{pi}"] = ("so0", cdump)
            for ek in ("ve", "sc", "te"):
                wait(ek, "so0", cdump)
        need("sc", "qbf")
        wait("sc", "t", tKV)
        sK = inc(nc.scalar.copy(out=qbf[:, 0:128], in_=ps[:, _bk(kb, 0, 128)]), "s")
        ksb = 4 + par
        need("te", f"bank{ksb}")
        wait("te", "s", sK)
        tKs = inc(nc.tensor.matmul(out=ps[:, _bk(ksb, 0, 128)], lhsT=rsw[:],
                                   rhs=qbf[:, 0:128], start=True, stop=True), "t")
        done("qbf", "t", tKs)
        tV = tKV
        wait("sc", "t", tV)
        sV = inc(nc.scalar.copy(out=vt[:, ch * VSTRIDE:ch * VSTRIDE + 128],
                                in_=ps[:, _bk(vb, 0, 128)]), "s")
        done(f"bank{vb}", "s", sV)
        if PREV_ROPE[0] is not None:
            emit_rope(*PREV_ROPE[0])
        PREV_ROPE[0] = (par, ch, ck, tKs)
    if PREV_ROPE[0] is not None:
        emit_rope(*PREV_ROPE[0])
        PREV_ROPE[0] = None
    te_barrier()
    for r in range(3):
        done(f"wq3_{r}", "t", CNT["t"])
    if DBG == "kv":
        wait("sy", "v", CNT["v"])
        wait("sy", "s", CNT["s"])
        inc(nc.sync.dma_start(out=ddbg[:, 0:PAD], in_=kf[:]), "so0")
        inc(nc.sync.dma_start(out=ddbg[:, 4096:4096 + KT * VSTRIDE], in_=vt[:]), "so0")
        inc(nc.sync.dma_start(out=ddbg[:, 8192:8192 + 4096],
                              in_=xn_fm[:, 0:4096]), "so0")
        wait("sy", "so0", CNT["so0"])

    if DBG == "xn":
        wait("sy", "v", CNT["v"])
        wait("sy", "s", CNT["s"])
        inc(nc.sync.dma_start(out=ddb3, in_=lnx[:]), "so0")
        wait("sy", "so0", CNT["so0"])

    xn3 = xn_fm[:].rearrange("p (d q) -> p d q", d=NDC)
    qo3h = qo_fm[:].rearrange("p (h q) -> p h q", h=H)
    for tau in range(0 if TRUNC == "p1" else NTILES):
        t_rows = HALO + tau * TILE

        wait("sy", "v", CNT["v"])
        wait("sy", "so0", CNT["so0"])
        wait("sy", "so1", CNT["so1"])
        load_x_chunk(t_rows, 0)
        for ch8 in range(TILE // 128):
            pi = ch8 % 2
            if ch8 + 1 < TILE // 128:
                load_x_chunk(t_rows + (ch8 + 1) * 128, (ch8 + 1) % 2)
            ln_chunk(pi)
            ev_s, ev_v = transposes(
                lambda db, _c=ch8: xn3[:, db * 4:(db + 1) * 4, _c * 128:(_c + 1) * 128], pi)
        te_barrier()
        wait("te", "s", ev_s)
        wait("te", "v", ev_v)

        need("sy", "csq")
        need("sy", "xbuf0")
        inc(nc.sync.dma_start(out=csq[:], in_=dcs[:, t_rows:t_rows + TILE]), "cq")
        cq = inc(nc.sync.dma_start(out=snq[:], in_=dsn[:, t_rows:t_rows + TILE]), "cq")

        def wq_load(h, half):
            reg = (2 * h + half) % 3
            need("gp", f"wq3_{reg}")
            c = inc(nc.gpsimd.dma_start(
                out=wtmp[:, reg * 2048:(reg + 1) * 2048],
                in_=dwq[h, :, half * 16:(half + 1) * 16, :].rearrange("p c k -> p (c k)")), f"wq{reg}")
            GUARD[f"wqld_{reg}"] = (f"wq{reg}", c)

        wq_load(0, 0)
        wq_load(0, 1)
        wq_load(1, 0)
        for h in range(H):
            tQ = 0
            for sg in range(2):
                qb = sg + 2 * (h % 2)
                need("te", f"bank{qb}")
                for d in range(NDC):
                    reg = (2 * h + d // 16) % 3
                    if d % 16 == 0:
                        gg = GUARD[f"wqld_{reg}"]
                        wait("te", gg[0], gg[1])
                    mm = nc.tensor.matmul(
                        out=ps[:, _bk(qb)],
                        lhsT=wtmp[:, reg * 2048 + (d % 16) * 128: reg * 2048 + (d % 16 + 1) * 128],
                        rhs=xn_fm[:, d * TILE + sg * 512: d * TILE + (sg + 1) * 512],
                        start=(d == 0), stop=(d == NDC - 1))
                tQ = inc(mm, "t")
                need("sc", "qbf")
                wait("sc", "t", tQ)
                sQ = inc(nc.scalar.copy(out=qbf[:], in_=ps[:, _bk(qb)]), "s")
                sw = 4 + sg
                need("te", f"bank{sw}")
                wait("te", "s", sQ)
                tSw = inc(nc.tensor.matmul(out=ps[:, _bk(sw)], lhsT=rsw[:],
                                           rhs=qbf[:], start=True, stop=True), "t")
                done("qbf", "t", tSw)
                wait("ve", "t", tSw)
                wait("ve", "cq", cq)
                if DBG == "qnorope":
                    vQ = inc(nc.vector.tensor_copy(
                        out=qo_fm[:, h * TILE + sg * 512: h * TILE + (sg + 1) * 512],
                        in_=ps[:, _bk(qb)]), "v")
                else:
                    nc.vector.tensor_tensor(out=rt1[:, 0:512], in0=ps[:, _bk(qb)],
                                            in1=csq[:, sg * 512:(sg + 1) * 512], op=ALU.mult)
                    nc.vector.tensor_tensor(out=rt2[:, 0:512], in0=ps[:, _bk(sw)],
                                            in1=snq[:, sg * 512:(sg + 1) * 512], op=ALU.mult)
                    vQ = inc(nc.vector.tensor_tensor(
                        out=qo_fm[:, h * TILE + sg * 512: h * TILE + (sg + 1) * 512],
                        in0=rt1[:, 0:512], in1=rt2[:, 0:512], op=ALU.add), "v")
                done(f"bank{qb}", "v", vQ)
                done(f"bank{sw}", "v", vQ)
            done(f"wq3_{(2 * h) % 3}", "t", tQ)
            done(f"wq3_{(2 * h + 1) % 3}", "t", tQ)
            if h + 1 < H:
                wq_load(h + 1, 1)
            if h + 2 < H:
                wq_load(h + 2, 0)
        done("csq", "v", CNT["v"])
        te_barrier()
        if DBG in ("qr", "qnorope") and tau == 0:
            wait("sy", "v", CNT["v"])
            inc(nc.sync.dma_start(out=ddbg, in_=qo_fm[:]), "so0")
            wait("sy", "so0", CNT["so0"])
        if TRUNC == "p2b":
            break

        VOS = {}
        for m in range(4):
            B = 4 * tau + m
            kt0 = 2 * B
            acol = 2 * B
            for h in range(H):
                par = h % 2
                sgb = par * 1536
                sxo = ((m * H + h) % 4) * 64
                sS_list = []
                for j in range(6):
                    bj = j % 3
                    cj = (j // 3) * 256
                    need("te", f"bank{bj}")
                    tS = inc(nc.tensor.matmul(
                        out=ps[:, _bk(bj, cj, 256)],
                        lhsT=kf[:, (kt0 + j) * 128:(kt0 + j + 1) * 128],
                        rhs=qo_fm[:, h * TILE + m * 256: h * TILE + (m + 1) * 256],
                        start=True, stop=True), "t")
                    wait("sc", "t", tS)
                    need("sc", f"sig{par}_{j}")
                    sS = inc(nc.scalar.activation(out=sig[:, sgb + j * 256:sgb + (j + 1) * 256],
                                                  in_=ps[:, _bk(bj, cj, 256)],
                                                  func=AF.Sigmoid, scale=SCALE), "s")
                    sS_list.append(sS)
                    done(f"bank{bj}", "s", sS)
                for qs in range(2):
                    bA = (3 + 2 * par) + qs
                    need("te", f"bank{bA}")
                    mm = None
                    for grp in range(3):
                        for jj in range(2):
                            j = grp * 2 + jj
                            wait("te", "s", sS_list[j])
                            mm = nc.tensor.matmul(
                                out=ps[:, _bk(bA, 132 * grp, 129)],
                                lhsT=sig[:, sgb + j * 256 + qs * 128: sgb + j * 256 + qs * 128 + 128],
                                rhs=vt[:, (kt0 + j) * VSTRIDE:(kt0 + j) * VSTRIDE + 129],
                                start=(jj == 0), stop=(jj == 1))
                    tA = inc(mm, "t")
                    if qs == 1:
                        for j in range(6):
                            done(f"sig{par}_{j}", "t", tA)
                    wait("ve", "t", tA)
                    nc.vector.tensor_copy(out=rt1[:, 0:129], in_=ps[:, _bk(bA, 132, 129)])
                    nc.vector.tensor_tensor(out=rt1[:, 132:261], in0=ps[:, _bk(bA, 0, 129)],
                                            in1=rt1[:, 0:129], op=ALU.add)
                    vA = inc(nc.vector.tensor_tensor(out=rt1[:, 264:393], in0=ps[:, _bk(bA, 264, 129)],
                                                     in1=rt1[:, 0:129], op=ALU.add), "v")
                    done(f"bank{bA}", "v", vA)
                    nc.vector.reciprocal(out=sx[:, sxo + 32:sxo + 33], in_=rt1[:, 260:261])
                    nc.vector.reciprocal(out=sx[:, sxo + 33:sxo + 34], in_=rt1[:, 392:393])
                    nc.vector.tensor_tensor(out=sx[:, sxo + 34:sxo + 35], in0=sx[:, sxo + 32:sxo + 33],
                                            in1=am_t[:, acol + qs:acol + qs + 1], op=ALU.mult)
                    nc.vector.tensor_tensor(out=sx[:, sxo + 35:sxo + 36], in0=sx[:, sxo + 33:sxo + 34],
                                            in1=al_t[:, acol + qs:acol + qs + 1], op=ALU.mult)
                    nc.vector.tensor_scalar(out=rt2[:, 0:128], in0=rt1[:, 132:260],
                                            scalar1=sx[:, sxo + 34:sxo + 35],
                                            scalar2=0.0, op0=ALU.mult, op1=ALU.add)
                    slot = par * 2 + qs
                    need("ve", f"attsl{slot}")
                    VOS[slot] = inc(nc.vector.scalar_tensor_tensor(
                        out=att[:, slot * 128:(slot + 1) * 128], in0=rt1[:, 264:392],
                        scalar=sx[:, sxo + 35:sxo + 36], in1=rt2[:, 0:128], op0=ALU.mult, op1=ALU.add), "v")
                if par == 1:
                    need("te", "bank7")
                    tT = None
                    for sl in range(4):
                        wait("te", "v", VOS[sl])
                        tT = nc.tensor.transpose(out=ps[:, _bk(7, sl * 128, 128)],
                                                 in_=att[:, sl * 128:(sl + 1) * 128],
                                                 identity=eye[:])
                    tTg = inc(tT, "t")
                    for sl in range(4):
                        done(f"attsl{sl}", "t", tTg)
                    dst = qo3h[:, h - 1:h + 1, m * 256:(m + 1) * 256]
                    if (h // 2) % 2 == 0:
                        wait("sc", "t", tTg)
                        c = inc(nc.scalar.copy(out=dst, in_=ps[:, _bk(7, 0, 512)]), "s")
                        done("bank7", "s", c)
                    else:
                        wait("ve", "t", tTg)
                        c = inc(nc.vector.tensor_copy(out=dst, in_=ps[:, _bk(7, 0, 512)]), "v")
                        done("bank7", "v", c)
        te_barrier()

        for r in range(3):
            gg = GUARD.get(f"wq3_{r}")
            if gg:
                wait("gp", gg[0], gg[1])
        g_war = inc(nc.gpsimd.dma_start(out=wtmp[:, 0:NDC * 136],
                                        in_=dwar.rearrange("p c k -> p (c k)")), "wl")
        g_fm_ready = 0
        for tt in range(TILE // 128):
            hb = tt % 2
            sxo = (tt % 4) * 64
            need("te", f"bank{hb}")
            wait("te", "wl", g_war)
            for d in range(NDC):
                mm = nc.tensor.matmul(out=ps[:, _bk(hb, 0, 136)],
                                      lhsT=xn_fm[:, d * TILE + tt * 128: d * TILE + (tt + 1) * 128],
                                      rhs=wtmp[:, d * 136:(d + 1) * 136],
                                      start=(d == 0), stop=(d == NDC - 1))
            tH = inc(mm, "t")
            wait("ve", "t", tH)
            wait("ve", "dc", c_consts)
            sbase = 256 + (tt % 2) * 128
            SC8 = lambda o: sx[:, sbase + o: sbase + o + 8]
            SC1 = lambda o: sx[:, sbase + o: sbase + o + 1]
            z0 = SC8(0); zs = SC8(8); zc = SC8(16)
            nc.vector.tensor_tensor(out=z0, in0=ps[:, _bk(hb, 128, 8)], in1=br_t[:], op=ALU.add)
            VF()
            nc.vector.tensor_copy(out=zs, in_=z0)
            VF()
            ces = [(0, 1), (2, 3), (4, 5), (6, 7), (0, 2), (1, 3), (4, 6), (5, 7),
                   (1, 2), (5, 6), (0, 4), (1, 5), (2, 6), (3, 7), (2, 4), (3, 5),
                   (1, 2), (3, 4), (5, 6)]
            for (i, j) in ces:
                nc.vector.tensor_tensor(out=SC1(84), in0=zs[:, i:i + 1],
                                        in1=zs[:, j:j + 1], op=ALU.max)
                nc.vector.tensor_tensor(out=zs[:, j:j + 1], in0=zs[:, i:i + 1],
                                        in1=zs[:, j:j + 1], op=ALU.min)
                nc.vector.tensor_copy(out=zs[:, i:i + 1], in_=SC1(84))
            nc.vector.tensor_tensor(out=SC1(72), in0=zs[:, 0:1], in1=zs[:, 1:2], op=ALU.add)
            nc.vector.tensor_tensor(out=SC1(73), in0=zs[:, 2:3], in1=zs[:, 3:4], op=ALU.add)
            nc.vector.tensor_tensor(out=SC1(74), in0=zs[:, 4:5], in1=zs[:, 5:6], op=ALU.add)
            nc.vector.tensor_tensor(out=SC1(75), in0=zs[:, 6:7], in1=zs[:, 7:8], op=ALU.add)
            nc.vector.tensor_copy(out=zc[:, 0:1], in_=zs[:, 0:1])
            nc.vector.tensor_tensor(out=SC1(76), in0=SC1(72), in1=SC1(73), op=ALU.add)
            nc.vector.tensor_copy(out=zc[:, 1:2], in_=SC1(72))
            nc.vector.tensor_tensor(out=zc[:, 2:3], in0=SC1(72), in1=zs[:, 2:3], op=ALU.add)
            nc.vector.tensor_copy(out=zc[:, 3:4], in_=SC1(76))
            nc.vector.tensor_tensor(out=SC1(77), in0=SC1(76), in1=SC1(74), op=ALU.add)
            nc.vector.tensor_tensor(out=zc[:, 4:5], in0=SC1(76), in1=zs[:, 4:5], op=ALU.add)
            nc.vector.tensor_copy(out=zc[:, 5:6], in_=SC1(77))
            nc.vector.tensor_tensor(out=zc[:, 6:7], in0=SC1(77), in1=zs[:, 6:7], op=ALU.add)
            nc.vector.tensor_tensor(out=zc[:, 7:8], in0=SC1(77), in1=SC1(75), op=ALU.add)
            nc.vector.tensor_tensor(out=SC8(24), in0=zs, in1=kv_t[:], op=ALU.mult)
            VF()
            nc.vector.tensor_scalar(out=SC8(32), in0=SC8(24), scalar1=1.0, scalar2=0.0,
                                    op0=ALU.add, op1=ALU.add)
            VF()
            sup = SC8(40)
            nc.vector.tensor_tensor(out=sup, in0=SC8(32), in1=zc, op=ALU.is_gt)
            VF()
            nc.vector.tensor_reduce(out=SC1(80), in_=sup, axis=AXX, op=ALU.add)
            dsel = SC8(48)
            nc.vector.tensor_copy(out=dsel[:, 7:8], in_=sup[:, 7:8])
            nc.vector.tensor_tensor(out=dsel[:, 0:7], in0=sup[:, 0:7], in1=sup[:, 1:8],
                                    op=ALU.subtract)
            VF()
            nc.vector.tensor_tensor(out=SC8(56), in0=zc, in1=dsel, op=ALU.mult)
            nc.vector.reciprocal(out=SC1(83), in_=SC1(80))
            nc.vector.tensor_reduce(out=SC1(81), in_=SC8(56), axis=AXX, op=ALU.add)
            VF()
            nc.vector.tensor_scalar(out=SC1(82), in0=SC1(81), scalar1=1.0, scalar2=0.0,
                                    op0=ALU.subtract, op1=ALU.add)
            VF()
            nc.vector.tensor_tensor(out=SC1(85), in0=SC1(82), in1=SC1(83), op=ALU.mult)
            VF()
            w8 = SC8(64)
            nc.vector.tensor_scalar(out=w8, in0=z0, scalar1=SC1(85), scalar2=0.0,
                                    op0=ALU.subtract, op1=ALU.max)
            VF()
            need("ve", "gpre")
            for e in range(8):
                mm = nc.vector.tensor_scalar(out=gpre[:, e * 16:(e + 1) * 16],
                                             in0=ps[:, _bk(hb, e * 16, 16)],
                                             scalar1=w8[:, e:e + 1], scalar2=0.0,
                                             op0=ALU.mult, op1=ALU.add)
            vG = inc(VF(), "v")
            done(f"bank{hb}", "v", vG)
            gb = 2 + tt % 2
            need("te", f"bank{gb}")
            wait("te", "v", vG)
            tG = inc(nc.tensor.transpose(out=ps[:, _bk(gb, 0, 128)], in_=gpre[:],
                                         identity=eye[:]), "t")
            done("gpre", "t", tG)
            need("sc", "g_fm")
            wait("sc", "t", tG)
            sG = inc(nc.scalar.copy(out=g_fm[:, tt * 128:(tt + 1) * 128],
                                    in_=ps[:, _bk(gb, 0, 128)]), "s")
            done(f"bank{gb}", "s", sG)
            g_fm_ready = sG
            GUARD["war_used"] = ("t", tH)
        te_barrier()

        need("gp", "war_used")
        g_lb = inc(nc.gpsimd.dma_start(out=wtmp[:, 0:4096], in_=dlb), "wl")
        need("sy", "xbuf0")
        rows0 = tau * TILE
        for oc in range(8):
            need("gp", "bbt")
            g_bb = inc(nc.gpsimd.dma_start(out=bbt[:], in_=dbb[:, oc * 512:(oc + 1) * 512]), "bb")

            def wob_load(d):
                reg = d % 2
                for nm, buf, dr in (("o", wob_o, dwo), ("b", wob_b, dwb)):
                    need("gp", f"wob{nm}{reg}")
                    c = inc(nc.gpsimd.dma_start(
                        out=buf[:, reg * 512:(reg + 1) * 512],
                        in_=dr[oc, :, d, :]), f"w{nm}{reg}")
                    GUARD[f"wobld{nm}{reg}"] = (f"w{nm}{reg}", c)

            wob_load(0)
            wob_load(1)
            for tt in range(2):
                need("sy", f"xres{tt}")
                c = inc(nc.sync.dma_start(
                    out=xres[:, tt * 512:(tt + 1) * 512],
                    in_=dx[HALO + rows0 + tt * 128: HALO + rows0 + (tt + 1) * 128,
                           oc * 512:(oc + 1) * 512]), f"xr{tt}")
                GUARD[f"xresld{tt}"] = (f"xr{tt}", c)
            for d in range(NDC):
                reg = d % 2
                for nm in ("o", "b"):
                    gg = GUARD[f"wobld{nm}{reg}"]
                    wait("te", gg[0], gg[1])
                mm = None
                for tt in range(8):
                    if d == 0:
                        need("te", f"bank{tt}")
                    nc.tensor.matmul(
                        out=ps[:, _bk(tt)],
                        lhsT=qo_fm[:, d * TILE + tt * 128: d * TILE + (tt + 1) * 128],
                        rhs=wob_o[:, reg * 512:(reg + 1) * 512],
                        start=(d == 0), stop=False)
                for tt in range(8):
                    mm = nc.tensor.matmul(
                        out=ps[:, _bk(tt)],
                        lhsT=xn_fm[:, d * TILE + tt * 128: d * TILE + (tt + 1) * 128],
                        rhs=wob_b[:, reg * 512:(reg + 1) * 512],
                        start=False, stop=False)
                tC = inc(mm, "t")
                done(f"wobo{reg}", "t", tC)
                done(f"wobb{reg}", "t", tC)
                if d + 2 < NDC:
                    wob_load(d + 2)
            wait("te", "wl", g_lb)
            wait("te", "bb", g_bb)
            wait("te", "s", g_fm_ready)
            wait("te", "dc", g_on)
            for tt in range(8):
                nc.tensor.matmul(out=ps[:, _bk(tt)],
                                 lhsT=g_fm[:, tt * 128:(tt + 1) * 128],
                                 rhs=wtmp[:, oc * 512:(oc + 1) * 512],
                                 start=False, stop=False)
            tlast = {}
            for tt in range(8):
                mm = nc.tensor.matmul(out=ps[:, _bk(tt)], lhsT=ones16[:],
                                      rhs=bbt[:], start=False, stop=True)
                tlast[tt] = inc(mm, "t")
            done("bbt", "t", tlast[7])
            for tt in range(8):
                r2 = tt % 2
                wait("ve", "t", tlast[tt])
                gg = GUARD[f"xresld{r2}"]
                wait("ve", gg[0], gg[1])
                need("ve", f"outst{r2}")
                vE = inc(nc.vector.tensor_tensor(out=outst[:, r2 * 512:(r2 + 1) * 512],
                                                 in0=ps[:, _bk(tt)],
                                                 in1=xres[:, r2 * 512:(r2 + 1) * 512],
                                                 op=ALU.add), "v")
                done(f"bank{tt}", "v", vE)
                done(f"xres{r2}", "v", vE)
                wait("sy", "v", vE)
                cS = inc(nc.sync.dma_start(
                    out=dout[rows0 + tt * 128: rows0 + (tt + 1) * 128, oc * 512:(oc + 1) * 512],
                    in_=outst[:, r2 * 512:(r2 + 1) * 512]), f"so{r2}")
                done(f"outst{r2}", f"so{r2}", cS)
                if tt + 2 < 8:
                    need("sy", f"xres{r2}")
                    c = inc(nc.sync.dma_start(
                        out=xres[:, r2 * 512:(r2 + 1) * 512],
                        in_=dx[HALO + rows0 + (tt + 2) * 128: HALO + rows0 + (tt + 3) * 128,
                               oc * 512:(oc + 1) * 512]), f"xr{r2}")
                    GUARD[f"xresld{r2}"] = (f"xr{r2}", c)
        te_barrier()
        t_all = CNT["t"]
        for r in range(3):
            done(f"wq3_{r}", "t", t_all)
        done("wtmp_lb", "t", t_all)
        done("g_fm", "t", t_all)
        done("qo_free", "t", t_all)

    nc.sync.wait_ge(sem["so0"], CNT["so0"])
    nc.sync.wait_ge(sem["so1"], CNT["so1"])
    fin_t = inc(nc.tensor.nop(), "t")
    fin_v = inc(nc.vector.nop(), "v")
    fin_s = inc(nc.scalar.nop(), "s")
    fin_q = inc(nc.sync.nop(), "s", 1)
    nc.gpsimd.wait_ge(sem["t"], fin_t)
    nc.gpsimd.wait_ge(sem["v"], fin_v)
    nc.gpsimd.wait_ge(sem["s"], fin_q)
    for k in DMAK:
        if CNT[k]:
            nc.gpsimd.wait_ge(sem[k], CNT[k])
    for k in semkeys:
        nc.gpsimd.sem_clear(sem[k])
    ctx.close()
    return nc



def _host_prep(inputs):
    x = np.asarray(inputs["L0_attn_x"], dtype=np.float32).reshape(S, D)
    Wq = np.asarray(inputs["L0_attn_Wq"], dtype=np.float32)
    Wk = np.asarray(inputs["L0_attn_Wk"], dtype=np.float32)
    Wv = np.asarray(inputs["L0_attn_Wv"], dtype=np.float32)
    Wo = np.asarray(inputs["L0_attn_Wo"], dtype=np.float32)
    Wb = np.asarray(inputs["L0_ffn_W_base"], dtype=np.float32)
    bb = np.asarray(inputs["L0_ffn_b_base"], dtype=np.float32)
    Wr = np.asarray(inputs["L0_ffn_W_router"], dtype=np.float32)
    br = np.asarray(inputs["L0_ffn_b_router"], dtype=np.float32)
    lA = np.asarray(inputs["L0_ffn_lora_A"], dtype=np.float32)
    lB = np.asarray(inputs["L0_ffn_lora_B"], dtype=np.float32)

    wq_arr = np.ascontiguousarray(Wq.reshape(H, DH, NDC, 128).transpose(0, 3, 2, 1))
    wo_arr = np.ascontiguousarray(Wo.reshape(8, 512, NDC, 128).transpose(0, 3, 2, 1))
    wb_arr = np.ascontiguousarray(Wb.reshape(8, 512, NDC, 128).transpose(0, 3, 2, 1))
    wk_arr = np.ascontiguousarray(Wk.reshape(128, NDC, 128).transpose(2, 1, 0))
    wv_arr = np.ascontiguousarray(Wv.reshape(128, NDC, 128).transpose(2, 1, 0))
    A_t = lA.transpose(1, 0, 2).reshape(D, 128)
    war = np.ascontiguousarray(np.concatenate([A_t, Wr.T], axis=1)
                               .reshape(NDC, 128, 136).transpose(1, 0, 2))
    lb2 = np.ascontiguousarray(lB.reshape(128, D))
    bb16 = np.zeros((16, D), np.float32)
    bb16[0] = bb
    br_rep = np.ascontiguousarray(np.broadcast_to(br[None, :], (128, 8)))
    kvec = np.ascontiguousarray(np.broadcast_to(np.arange(1, 9, dtype=np.float32)[None, :],
                                                (128, 8)))
    eye = np.eye(128, dtype=np.float32)
    eyb = np.eye(128, dtype=np.float32).astype(ml_dtypes.bfloat16)
    rsw = np.zeros((128, 128), np.float32)
    for i in range(128):
        rsw[i, (i + 64) % 128] = 1.0
    rsw = rsw.astype(ml_dtypes.bfloat16)
    on16 = np.zeros((16, 128), np.float32)
    on16[0, :] = 1.0
    on16 = on16.astype(ml_dtypes.bfloat16)

    inv_freq = 1.0 / 10000.0 ** (np.arange(0, DH, 2, dtype=np.float32) / DH)
    t = np.arange(S, dtype=np.float32)
    f = np.outer(t, inv_freq)
    cosf = np.zeros((S, DH), dtype=np.float32)
    sinf = np.zeros((S, DH), dtype=np.float32)
    cosf[:, 0::2] = np.cos(f); cosf[:, 1::2] = np.cos(f)
    sinf[:, 0::2] = np.sin(f); sinf[:, 1::2] = np.sin(f)
    cosT = np.ascontiguousarray(cosf.T)
    sinT = sinf.T.copy()
    sinT[0:64, :] *= -1.0

    alpha = np.linspace(0.0, 1.0, 256, dtype=np.float32)

    in_maps = []
    for c in range(NCORES):
        lo = c * TC - HALO
        xp = np.zeros((PAD, D), np.float32)
        s0, s1 = max(lo, 0), min(lo + PAD, S)
        xp[s0 - lo: s1 - lo] = x[s0:s1]
        cs_c = np.zeros((128, PAD), np.float32)
        sn_c = np.zeros((128, PAD), np.float32)
        cs_c[:, s0 - lo: s1 - lo] = cosT[:, s0:s1]
        sn_c[:, s0 - lo: s1 - lo] = sinT[:, s0:s1]
        al_t = np.empty((128, 16), np.float32)
        am_t = np.empty((128, 16), np.float32)
        for b in range(8):
            for qs in range(2):
                col = 2 * b + qs
                a = alpha[qs * 128:(qs + 1) * 128]
                if c == 0 and b == 0:
                    al_t[:, col] = 1.0
                    am_t[:, col] = 0.0
                else:
                    al_t[:, col] = a
                    am_t[:, col] = 1.0 - a
        vc_t = np.zeros((128, KT), np.float32)
        for kt in range(KT):
            g = lo + kt * 128 + np.arange(128)
            vc_t[:, kt] = ((g >= 0) & (g < S)).astype(np.float32)
        in_maps.append({
            "xp": xp, "wq": wq_arr, "wo": wo_arr, "wb": wb_arr, "wk": wk_arr,
            "wv": wv_arr, "war": war, "lb": lb2, "bb": bb16, "cs": cs_c, "sn": sn_c,
            "al": al_t, "am": am_t, "vc": vc_t, "br": br_rep, "kv": kvec,
            "ey": eye, "eyb": eyb, "rs": rsw, "on16": on16,
        })
    return in_maps


def kernel(**inputs):
    in_maps = _host_prep(inputs)
    nc = build_nc()
    res = run_bass_kernel_spmd(nc, in_maps, core_ids=list(range(NCORES)))
    out = np.concatenate([res.results[c]["out"] for c in range(NCORES)], axis=0)
    return np.ascontiguousarray(out.reshape(1, S, D), dtype=np.float32)


# version 16
# speedup vs baseline: 1.3356x; 1.0693x over previous
import math
import contextlib
import numpy as np
import ml_dtypes
import concourse.bass as bass
import concourse.mybir as mybir
from concourse.bass_utils import run_bass_kernel_spmd

F32 = mybir.dt.float32
BF16 = mybir.dt.bfloat16
AF = mybir.ActivationFunctionType
ALU = mybir.AluOpType
AXX = mybir.AxisListType.X

NCORES = 8
S, D, H, DH = 16384, 4096, 32, 128
TC = 2048
HALO = 256
PAD = TC + 2 * HALO
TILE = 1024
NTILES = TC // TILE
NCH1 = PAD // 128
KT = 20
VSTRIDE = 132
SCALE = 1.0 / math.sqrt(DH)
EPS = 1e-5
NDC = D // 128
TRUNC = None
KV_ILV = True
DBG = None


def _bk(b, off=0, width=512):
    return slice(512 * b + off, 512 * b + off + width)


def build_nc():
    nc = bass.Bass("TRN2", target_bir_lowering=False)

    dx = nc.dram_tensor("xp", [PAD, D], F32, kind="ExternalInput").ap()
    dwq = nc.dram_tensor("wq", [H, 128, NDC, 128], F32, kind="ExternalInput").ap()
    dwo = nc.dram_tensor("wo", [8, 128, NDC, 512], F32, kind="ExternalInput").ap()
    dwb = nc.dram_tensor("wb", [8, 128, NDC, 512], F32, kind="ExternalInput").ap()
    dwk = nc.dram_tensor("wk", [128, NDC, 128], F32, kind="ExternalInput").ap()
    dwv = nc.dram_tensor("wv", [128, NDC, 128], F32, kind="ExternalInput").ap()
    dwar = nc.dram_tensor("war", [128, NDC, 136], F32, kind="ExternalInput").ap()
    dlb = nc.dram_tensor("lb", [128, D], F32, kind="ExternalInput").ap()
    dbb = nc.dram_tensor("bb", [16, D], F32, kind="ExternalInput").ap()
    dcs = nc.dram_tensor("cs", [128, PAD], F32, kind="ExternalInput").ap()
    dsn = nc.dram_tensor("sn", [128, PAD], F32, kind="ExternalInput").ap()
    dal = nc.dram_tensor("al", [128, 16], F32, kind="ExternalInput").ap()
    dam = nc.dram_tensor("am", [128, 16], F32, kind="ExternalInput").ap()
    dvc = nc.dram_tensor("vc", [128, KT], F32, kind="ExternalInput").ap()
    dbr = nc.dram_tensor("br", [128, 8], F32, kind="ExternalInput").ap()
    dkv = nc.dram_tensor("kv", [128, 8], F32, kind="ExternalInput").ap()
    dey = nc.dram_tensor("ey", [128, 128], F32, kind="ExternalInput").ap()
    deyb = nc.dram_tensor("eyb", [128, 128], BF16, kind="ExternalInput").ap()
    drs = nc.dram_tensor("rs", [128, 128], BF16, kind="ExternalInput").ap()
    don = nc.dram_tensor("on16", [16, 128], BF16, kind="ExternalInput").ap()
    dout = nc.dram_tensor("out", [TC, D], F32, kind="ExternalOutput").ap()
    ddbg = nc.dram_tensor("dbg", [128, NDC * TILE], BF16, kind="ExternalOutput").ap() if DBG and DBG != "xn" else None
    if DBG == "xn":
        ddbg = nc.dram_tensor("dbg", [128, NCH1 * D], BF16, kind="ExternalOutput").ap()
        ddb2 = nc.dram_tensor("dbg2", [128, NCH1 * D], F32, kind="ExternalOutput").ap()
        GT = NCH1 + (TC // TILE) * (TILE // 128)
        ddb3 = nc.dram_tensor("dbg3", [128, GT * 8], F32, kind="ExternalOutput").ap()

    ctx = contextlib.ExitStack()
    sb = lambda name, shape, dt: ctx.enter_context(nc.sbuf_tensor(name, shape, dt))

    xn_fm = sb("xn_fm", [128, NDC * TILE], BF16)
    qo_fm = sb("qo_fm", [128, NDC * TILE], BF16)
    kf = sb("kf", [128, PAD], BF16)
    vt = sb("vt", [128, KT * VSTRIDE], BF16)

    wtmp = sb("wtmp", [128, 6144], BF16)
    wob_o = sb("wob_o", [128, 1024], BF16)
    wob_b = sb("wob_b", [128, 1024], BF16)
    xb = sb("xb", [128, 4096], F32)
    xb2 = sb("xb2", [128, 4096], F32)
    XBS = [xb, xb2]
    csq = xb[:, 0:1024]
    snq = xb[:, 1024:2048]
    xres = xb[:, 2048:3072]
    outst = xb[:, 3072:4096]
    sig = sb("sig", [128, 2 * 6 * 256], BF16)
    qbf = sb("qbf", [128, 512], BF16)
    rt1 = sb("rt1", [128, 512], F32)
    rt2 = sb("rt2", [128, 512], F32)
    gpre = sb("gpre", [128, 128], F32)
    g_fm = sb("g_fm", [128, 1024], BF16)
    sx = sb("sx", [128, 512], F32)
    csk_sl = lambda par: sx[:, par * 128:(par + 1) * 128]
    snk_sl = lambda par: sx[:, 256 + par * 128:256 + (par + 1) * 128]
    ln_full = sb("ln", [128, 128], F32)
    eye = sb("eye", [128, 128], F32)
    eyeb = sb("eyeb", [128, 128], BF16)
    att = sb("att", [128, 512], F32)
    rsw = sb("rsw", [128, 128], BF16)
    ones16 = sb("ones16", [16, 128], BF16)
    bbt = sb("bbt", [16, 512], BF16)
    al_t = sb("al_t", [128, 16], F32)
    am_t = sb("am_t", [128, 16], F32)
    vc_t = sb("vc_t", [128, KT], F32)
    br_t = sb("br_t", [128, 8], F32)
    kv_t = sb("kv_t", [128, 8], F32)
    G_TOTAL = NCH1 + NTILES * (TILE // 128)
    lnx = sb("lnx", [128, G_TOTAL * 8], F32)
    flr = sb("flr", [128, 8], F32)
    ps = ctx.enter_context(nc.psum_tensor("ps", [128, 4096], F32))

    semkeys = ("dc", "ck0", "ck1", "cq", "x0", "x1", "x2", "x3", "xr0", "xr1", "so0", "so1",
               "wi", "wq0", "wq1", "wq2", "wl", "bb", "wo0", "wo1", "wo2", "wb0", "wb1", "wb2",
               "t", "v", "s", "gc")
    sem = {k: ctx.enter_context(nc.semaphore("sem_" + k)) for k in semkeys}
    CNT = {k: 0 for k in sem}
    LASTW = {}
    ENG = {"sy": nc.sync, "gp": nc.gpsimd, "te": nc.tensor, "ve": nc.vector, "sc": nc.scalar}

    DMAK = ("dc", "ck0", "ck1", "cq", "x0", "x1", "x2", "x3", "xr0", "xr1", "so0", "so1",
            "wi", "wq0", "wq1", "wq2", "wl", "bb", "wo0", "wo1", "wo2", "wb0", "wb1", "wb2")
    def inc(instr, key, n=None):
        n = n if n is not None else (16 if key in DMAK else 1)
        CNT[key] += n
        instr.then_inc(sem[key], n)
        return CNT[key]

    def wait(ek, key, val):
        if val <= 0:
            return
        if LASTW.get((ek, key), 0) >= val:
            return
        ENG[ek].wait_ge(sem[key], val)
        LASTW[(ek, key)] = val

    GUARD = {}

    def need(ek, res):
        g = GUARD.get(res)
        if g:
            wait(ek, g[0], g[1])

    def done(res, key, cnt):
        GUARD[res] = (key, cnt)

    def VF():
        return nc.vector.tensor_tensor(out=flr[:, 0:1], in0=flr[:, 4:5],
                                       in1=flr[:, 4:5], op=ALU.add)

    def te_barrier():
        wait("te", "s", CNT["s"])
        wait("te", "v", CNT["v"])

    LNDUMP = [0]
    LNGEN = [0]
    for k in semkeys:
        nc.gpsimd.sem_clear(sem[k])
    nc.all_engine_barrier()

    inc(nc.sync.dma_start(out=eye[:], in_=dey), "dc")
    inc(nc.sync.dma_start(out=eyeb[:], in_=deyb), "dc")
    inc(nc.sync.dma_start(out=rsw[:], in_=drs), "dc")
    inc(nc.sync.dma_start(out=al_t[:], in_=dal), "dc")
    inc(nc.sync.dma_start(out=am_t[:], in_=dam), "dc")
    inc(nc.sync.dma_start(out=vc_t[:], in_=dvc), "dc")
    inc(nc.sync.dma_start(out=br_t[:], in_=dbr), "dc")
    c_consts = inc(nc.sync.dma_start(out=kv_t[:], in_=dkv), "dc")
    WV0 = 24576
    g_wk = inc(nc.gpsimd.dma_start(out=wtmp[:, 0:4096], in_=dwk.rearrange("p c k -> p (c k)")), "wi")
    g_wv = inc(nc.gpsimd.dma_start(out=qo_fm[:, WV0:WV0 + 4096],
                                   in_=dwv.rearrange("p c k -> p (c k)")), "wi")
    c_consts = g_on = inc(nc.sync.dma_start(out=ones16[:], in_=don), "dc")
    wait("ve", "dc", c_consts)
    for _kt in range(KT):
        inc(nc.vector.tensor_copy(out=vt[:, _kt * VSTRIDE + 128:_kt * VSTRIDE + 129],
                                  in_=vc_t[:, _kt:_kt + 1]), "v")
    nc.vector.memset(flr[:], 0.0)
    nc.vector.memset(qbf[:], 0.0)
    nc.vector.memset(sig[:], 0.0)
    nc.vector.memset(rt1[:], 0.0)
    inc(nc.vector.memset(rt2[:], 0.0), "v")

    def ln_chunk(pi):
        buf = XBS[pi]
        """LN the 128-token chunk in buf (two 2048 pieces) in place.
        All stats cells are stride-4 (one SBUF line each) and rotate over 4
        generations to dodge the cross-engine read-line staleness. A cell is
        written at most once per generation before any read."""
        LNGEN[0] += 1
        gen = (LNGEN[0] % 2) * 64
        L = lambda i: ln_full[:, gen + 4 * i: gen + 4 * i + 1]
        need("ve", "lnbuf")
        for j in range(2):
            for qq4 in (2 * j, 2 * j + 1):
                gq = GUARD[f"xload{pi}_{qq4}"]
                wait("ve", gq[0], gq[1])
            nc.vector.tensor_reduce(out=L(j), in_=buf[:, j * 2048:(j + 1) * 2048],
                                    axis=AXX, op=ALU.add)
            for q in range(4):
                qq = j * 2048 + q * 512
                nc.vector.tensor_tensor(out=rt1[:, 0:512], in0=buf[:, qq:qq + 512],
                                        in1=buf[:, qq:qq + 512], op=ALU.mult)
                nc.vector.tensor_reduce(out=L(2 + j * 4 + q), in_=rt1[:, 0:512], axis=AXX, op=ALU.add)
        nc.vector.tensor_tensor(out=L(10), in0=L(2), in1=L(3), op=ALU.add)
        nc.vector.tensor_tensor(out=L(11), in0=L(4), in1=L(5), op=ALU.add)
        nc.vector.tensor_tensor(out=L(12), in0=L(6), in1=L(7), op=ALU.add)
        nc.vector.tensor_tensor(out=L(13), in0=L(8), in1=L(9), op=ALU.add)
        nc.vector.tensor_tensor(out=L(14), in0=L(10), in1=L(11), op=ALU.add)
        nc.vector.tensor_tensor(out=L(15), in0=L(12), in1=L(13), op=ALU.add)
        nc.vector.tensor_tensor(out=L(11), in0=L(0), in1=L(1), op=ALU.add)
        nc.vector.tensor_tensor(out=L(10), in0=L(14), in1=L(15), op=ALU.add)
        nc.vector.tensor_scalar(out=L(4), in0=L(11), scalar1=1.0 / D, scalar2=0.0,
                                op0=ALU.mult, op1=ALU.add)
        nc.vector.tensor_scalar(out=L(5), in0=L(10), scalar1=1.0 / D, scalar2=0.0,
                                op0=ALU.mult, op1=ALU.add)
        nc.vector.tensor_tensor(out=L(6), in0=L(4), in1=L(4), op=ALU.mult)
        VF()
        nc.vector.tensor_tensor(out=L(7), in0=L(5), in1=L(6), op=ALU.subtract)
        VF()
        nc.vector.tensor_scalar(out=L(8), in0=L(7), scalar1=EPS, scalar2=0.0,
                                op0=ALU.add, op1=ALU.add)
        VF()
        gg = LNGEN[0] - 1
        rcp = lnx[:, 8 * gg: 8 * gg + 1]
        rstd = lnx[:, 8 * gg + 4: 8 * gg + 5]
        nc.vector.reciprocal(out=rcp, in_=L(8))
        vR = inc(VF(), "v")
        wait("sc", "v", vR)
        sR = inc(nc.scalar.activation(out=rstd, in_=rcp, func=AF.Sqrt), "s")
        wait("ve", "s", sR)
        nc.vector.tensor_tensor(out=ln_full[:, gen + 56:gen + 57], in0=L(4), in1=rstd, op=ALU.mult)
        nmr = ln_full[:, gen + 52:gen + 53]
        VF()
        nc.vector.tensor_scalar(out=nmr, in0=ln_full[:, gen + 56:gen + 57], scalar1=-1.0,
                                scalar2=0.0, op0=ALU.mult, op1=ALU.add)
        VF()
        nc.vector.tensor_scalar(out=buf[:, 0:2048], in0=buf[:, 0:2048], scalar1=rstd,
                                scalar2=nmr, op0=ALU.mult, op1=ALU.add)
        vN = inc(nc.vector.tensor_scalar(out=buf[:, 2048:4096], in0=buf[:, 2048:4096],
                                         scalar1=rstd, scalar2=nmr,
                                         op0=ALU.mult, op1=ALU.add), "v")
        GUARD["xnorm"] = ("v", vN)


    def transposes(dest4, pi, kv=None):
        buf = XBS[pi]
        vN = GUARD["xnorm"][1]
        wait("te", "dc", c_consts)
        wait("te", "v", vN)
        if kv is not None:
            need("te", f"bank{kv[0]}")
            need("te", f"bank{kv[1]}")
            wait("te", "wi", g_wv)
        last_s = last_v = 0
        mmv = None

        def kvmm(dk):
            nc.tensor.matmul(out=ps[:, _bk(kv[0], 0, 128)],
                             lhsT=wtmp[:, dk * 128:(dk + 1) * 128],
                             rhs=xn_fm[:, dk * 128:(dk + 1) * 128],
                             start=(dk == 0), stop=(dk == NDC - 1))
            return nc.tensor.matmul(out=ps[:, _bk(kv[1], 0, 128)],
                                    lhsT=xn_fm[:, dk * 128:(dk + 1) * 128],
                                    rhs=qo_fm[:, WV0 + dk * 128:WV0 + (dk + 1) * 128],
                                    start=(dk == 0), stop=(dk == NDC - 1))

        NG = NDC // 4
        for db in range(NG):
            b = db % 2
            need("te", f"trb{b}")
            tT = None
            for q in range(4):
                d = db * 4 + q
                tT = nc.tensor.transpose(out=ps[:, _bk(b, q * 128, 128)],
                                         in_=buf[:, d * 128:(d + 1) * 128],
                                         identity=eye[:])
            tT = inc(tT, "t")
            wait("sc", "t", tT)
            c = inc(nc.scalar.copy(out=dest4(db), in_=ps[:, _bk(b, 0, 512)]), "s")
            done(f"trb{b}", "s", c)
            GUARD[f"evgrp{b}"] = ("s", c)
            last_s = c
            if kv is not None and db >= 1:
                g2 = GUARD[f"evgrp{(db - 1) % 2}"]
                wait("te", g2[0], g2[1])
                for dk in range((db - 1) * 4, db * 4):
                    mmv = kvmm(dk)
        if kv is not None:
            g2 = GUARD[f"evgrp{(NG - 1) % 2}"]
            wait("te", g2[0], g2[1])
            for dk in range(NDC - 4, NDC):
                mmv = kvmm(dk)
            GUARD["kvmm"] = ("t", inc(mmv, "t"))
        done(f"xbuf{pi}", "t", CNT["t"])
        return last_s, last_v

    def load_x_chunk(row, pi):
        buf = XBS[pi]
        need("sy", f"xbuf{pi}")
        for j in range(4):
            c = inc(nc.sync.dma_start(out=buf[:, j * 1024:(j + 1) * 1024],
                                     in_=dx[row:row + 128, j * 1024:(j + 1) * 1024]), f"x{j}")
            GUARD[f"xload{pi}_{j}"] = (f"x{j}", c)

    PREV_ROPE = [None]

    def emit_rope(par, ch, ck, tKs):
        kb_, ksb_ = 2 + par, 4 + par
        wait("ve", "t", tKs)
        wait("ve", f"ck{par}", ck)
        nc.vector.tensor_tensor(out=rt1[:, 0:128], in0=ps[:, _bk(kb_, 0, 128)],
                                in1=csk_sl(par), op=ALU.mult)
        nc.vector.tensor_tensor(out=rt2[:, 0:128], in0=ps[:, _bk(ksb_, 0, 128)],
                                in1=snk_sl(par), op=ALU.mult)
        vR = inc(nc.vector.tensor_tensor(out=kf[:, ch * 128:(ch + 1) * 128],
                                         in0=rt1[:, 0:128], in1=rt2[:, 0:128], op=ALU.add), "v")
        done(f"bank{kb_}", "v", vR)
        done(f"bank{ksb_}", "v", vR)
        done(f"csk{par}", "v", vR)

    load_x_chunk(0, 0)
    for ch in range(NCH1):
        par = ch % 2
        pi = ch % 2
        need("sy", f"csk{par}")
        ck = inc(nc.sync.dma_start(out=csk_sl(par),
                                   in_=dcs[:, ch * 128:(ch + 1) * 128]), f"ck{par}")
        ck = inc(nc.sync.dma_start(out=snk_sl(par),
                                   in_=dsn[:, ch * 128:(ch + 1) * 128]), f"ck{par}")
        if ch + 1 < NCH1:
            load_x_chunk((ch + 1) * 128, (ch + 1) % 2)
        ln_chunk(pi)
        kb = 2 + par
        vb = 6 + par
        ev_s, ev_v = transposes(lambda db: xn_fm[:, db * 512:(db + 1) * 512], pi,
                                kv=(kb, vb) if KV_ILV else None)
        if not KV_ILV:
            need("te", f"bank{kb}")
            need("te", f"bank{vb}")
            wait("te", "s", ev_s)
            wait("te", "v", ev_v)
            wait("te", "wi", g_wv)
            for d in range(NDC):
                nc.tensor.matmul(out=ps[:, _bk(kb, 0, 128)],
                                 lhsT=wtmp[:, d * 128:(d + 1) * 128],
                                 rhs=xn_fm[:, d * 128:(d + 1) * 128],
                                 start=(d == 0), stop=(d == NDC - 1))
            for d in range(NDC):
                mm = nc.tensor.matmul(out=ps[:, _bk(vb, 0, 128)],
                                      lhsT=xn_fm[:, d * 128:(d + 1) * 128],
                                      rhs=qo_fm[:, WV0 + d * 128:WV0 + (d + 1) * 128],
                                      start=(d == 0), stop=(d == NDC - 1))
            GUARD["kvmm"] = ("t", inc(mm, "t"))
        tKV = GUARD["kvmm"][1]
        if DBG == "xn":
            wait("sy", "s", ev_s)
            wait("sy", "v", ev_v)
            inc(nc.sync.dma_start(out=ddbg[:, ch * D:(ch + 1) * D],
                                  in_=xn_fm[:, 0:D]), "so0")
            cdump = inc(nc.sync.dma_start(out=ddb2[:, ch * D:(ch + 1) * D],
                                          in_=XBS[pi][:, 0:D]), "so0")
            GUARD[f"xbuf{pi}"] = ("so0", cdump)
            for ek in ("ve", "sc", "te"):
                wait(ek, "so0", cdump)
        need("sc", "qbf")
        wait("sc", "t", tKV)
        sK = inc(nc.scalar.copy(out=qbf[:, 0:128], in_=ps[:, _bk(kb, 0, 128)]), "s")
        ksb = 4 + par
        need("te", f"bank{ksb}")
        wait("te", "s", sK)
        tKs = inc(nc.tensor.matmul(out=ps[:, _bk(ksb, 0, 128)], lhsT=rsw[:],
                                   rhs=qbf[:, 0:128], start=True, stop=True), "t")
        done("qbf", "t", tKs)
        tV = tKV
        wait("sc", "t", tV)
        sV = inc(nc.scalar.copy(out=vt[:, ch * VSTRIDE:ch * VSTRIDE + 128],
                                in_=ps[:, _bk(vb, 0, 128)]), "s")
        done(f"bank{vb}", "s", sV)
        if PREV_ROPE[0] is not None:
            emit_rope(*PREV_ROPE[0])
        PREV_ROPE[0] = (par, ch, ck, tKs)
    if PREV_ROPE[0] is not None:
        emit_rope(*PREV_ROPE[0])
        PREV_ROPE[0] = None
    te_barrier()
    for r in range(3):
        done(f"wq3_{r}", "t", CNT["t"])
    if DBG == "kv":
        wait("sy", "v", CNT["v"])
        wait("sy", "s", CNT["s"])
        inc(nc.sync.dma_start(out=ddbg[:, 0:PAD], in_=kf[:]), "so0")
        inc(nc.sync.dma_start(out=ddbg[:, 4096:4096 + KT * VSTRIDE], in_=vt[:]), "so0")
        inc(nc.sync.dma_start(out=ddbg[:, 8192:8192 + 4096],
                              in_=xn_fm[:, 0:4096]), "so0")
        wait("sy", "so0", CNT["so0"])

    if DBG == "xn":
        wait("sy", "v", CNT["v"])
        wait("sy", "s", CNT["s"])
        inc(nc.sync.dma_start(out=ddb3, in_=lnx[:]), "so0")
        wait("sy", "so0", CNT["so0"])

    xn3 = xn_fm[:].rearrange("p (d q) -> p d q", d=NDC)
    qo3h = qo_fm[:].rearrange("p (h q) -> p h q", h=H)
    for tau in range(0 if TRUNC == "p1" else NTILES):
        t_rows = HALO + tau * TILE

        wait("sy", "v", CNT["v"])
        wait("sy", "so0", CNT["so0"])
        wait("sy", "so1", CNT["so1"])
        load_x_chunk(t_rows, 0)
        for ch8 in range(TILE // 128):
            pi = ch8 % 2
            if ch8 + 1 < TILE // 128:
                load_x_chunk(t_rows + (ch8 + 1) * 128, (ch8 + 1) % 2)
            ln_chunk(pi)
            ev_s, ev_v = transposes(
                lambda db, _c=ch8: xn3[:, db * 4:(db + 1) * 4, _c * 128:(_c + 1) * 128], pi)
        te_barrier()
        wait("te", "s", ev_s)
        wait("te", "v", ev_v)

        need("sy", "csq")
        need("sy", "xbuf0")
        inc(nc.sync.dma_start(out=csq[:], in_=dcs[:, t_rows:t_rows + TILE]), "cq")
        cq = inc(nc.sync.dma_start(out=snq[:], in_=dsn[:, t_rows:t_rows + TILE]), "cq")

        def wq_load(h, half):
            reg = (2 * h + half) % 3
            need("gp", f"wq3_{reg}")
            c = inc(nc.gpsimd.dma_start(
                out=wtmp[:, reg * 2048:(reg + 1) * 2048],
                in_=dwq[h, :, half * 16:(half + 1) * 16, :].rearrange("p c k -> p (c k)")), f"wq{reg}")
            GUARD[f"wqld_{reg}"] = (f"wq{reg}", c)

        wq_load(0, 0)
        wq_load(0, 1)
        wq_load(1, 0)
        for h in range(H):
            tQ = 0
            for sg in range(2):
                qb = sg + 2 * (h % 2)
                need("te", f"bank{qb}")
                for d in range(NDC):
                    reg = (2 * h + d // 16) % 3
                    if d % 16 == 0:
                        gg = GUARD[f"wqld_{reg}"]
                        wait("te", gg[0], gg[1])
                    mm = nc.tensor.matmul(
                        out=ps[:, _bk(qb)],
                        lhsT=wtmp[:, reg * 2048 + (d % 16) * 128: reg * 2048 + (d % 16 + 1) * 128],
                        rhs=xn_fm[:, d * TILE + sg * 512: d * TILE + (sg + 1) * 512],
                        start=(d == 0), stop=(d == NDC - 1))
                tQ = inc(mm, "t")
                need("sc", "qbf")
                wait("sc", "t", tQ)
                sQ = inc(nc.scalar.copy(out=qbf[:], in_=ps[:, _bk(qb)]), "s")
                sw = 4 + sg
                need("te", f"bank{sw}")
                wait("te", "s", sQ)
                tSw = inc(nc.tensor.matmul(out=ps[:, _bk(sw)], lhsT=rsw[:],
                                           rhs=qbf[:], start=True, stop=True), "t")
                done("qbf", "t", tSw)
                wait("ve", "t", tSw)
                wait("ve", "cq", cq)
                if DBG == "qnorope":
                    vQ = inc(nc.vector.tensor_copy(
                        out=qo_fm[:, h * TILE + sg * 512: h * TILE + (sg + 1) * 512],
                        in_=ps[:, _bk(qb)]), "v")
                else:
                    nc.vector.tensor_tensor(out=rt1[:, 0:512], in0=ps[:, _bk(qb)],
                                            in1=csq[:, sg * 512:(sg + 1) * 512], op=ALU.mult)
                    nc.vector.tensor_tensor(out=rt2[:, 0:512], in0=ps[:, _bk(sw)],
                                            in1=snq[:, sg * 512:(sg + 1) * 512], op=ALU.mult)
                    vQ = inc(nc.vector.tensor_tensor(
                        out=qo_fm[:, h * TILE + sg * 512: h * TILE + (sg + 1) * 512],
                        in0=rt1[:, 0:512], in1=rt2[:, 0:512], op=ALU.add), "v")
                done(f"bank{qb}", "v", vQ)
                done(f"bank{sw}", "v", vQ)
            done(f"wq3_{(2 * h) % 3}", "t", tQ)
            done(f"wq3_{(2 * h + 1) % 3}", "t", tQ)
            if h + 1 < H:
                wq_load(h + 1, 1)
            if h + 2 < H:
                wq_load(h + 2, 0)
        done("csq", "v", CNT["v"])
        te_barrier()
        if DBG in ("qr", "qnorope") and tau == 0:
            wait("sy", "v", CNT["v"])
            inc(nc.sync.dma_start(out=ddbg, in_=qo_fm[:]), "so0")
            wait("sy", "so0", CNT["so0"])
        if TRUNC == "p2b":
            break

        VOS = {}
        for m in range(4):
            B = 4 * tau + m
            kt0 = 2 * B
            acol = 2 * B
            for h in range(H):
                par = h % 2
                sgb = par * 1536
                sxo = ((m * H + h) % 4) * 64
                sS_list = []
                for j in range(6):
                    bj = j % 3
                    cj = (j // 3) * 256
                    need("te", f"bank{bj}")
                    tS = inc(nc.tensor.matmul(
                        out=ps[:, _bk(bj, cj, 256)],
                        lhsT=kf[:, (kt0 + j) * 128:(kt0 + j + 1) * 128],
                        rhs=qo_fm[:, h * TILE + m * 256: h * TILE + (m + 1) * 256],
                        start=True, stop=True), "t")
                    wait("sc", "t", tS)
                    need("sc", f"sig{par}_{j}")
                    sS = inc(nc.scalar.activation(out=sig[:, sgb + j * 256:sgb + (j + 1) * 256],
                                                  in_=ps[:, _bk(bj, cj, 256)],
                                                  func=AF.Sigmoid, scale=SCALE), "s")
                    sS_list.append(sS)
                    done(f"bank{bj}", "s", sS)
                for qs in range(2):
                    bA = (3 + 2 * par) + qs
                    need("te", f"bank{bA}")
                    mm = None
                    for grp in range(3):
                        for jj in range(2):
                            j = grp * 2 + jj
                            wait("te", "s", sS_list[j])
                            mm = nc.tensor.matmul(
                                out=ps[:, _bk(bA, 132 * grp, 129)],
                                lhsT=sig[:, sgb + j * 256 + qs * 128: sgb + j * 256 + qs * 128 + 128],
                                rhs=vt[:, (kt0 + j) * VSTRIDE:(kt0 + j) * VSTRIDE + 129],
                                start=(jj == 0), stop=(jj == 1))
                    tA = inc(mm, "t")
                    if qs == 1:
                        for j in range(6):
                            done(f"sig{par}_{j}", "t", tA)
                    wait("ve", "t", tA)
                    nc.vector.tensor_copy(out=rt1[:, 0:129], in_=ps[:, _bk(bA, 132, 129)])
                    nc.vector.tensor_tensor(out=rt1[:, 132:261], in0=ps[:, _bk(bA, 0, 129)],
                                            in1=rt1[:, 0:129], op=ALU.add)
                    vA = inc(nc.vector.tensor_tensor(out=rt1[:, 264:393], in0=ps[:, _bk(bA, 264, 129)],
                                                     in1=rt1[:, 0:129], op=ALU.add), "v")
                    done(f"bank{bA}", "v", vA)
                    nc.vector.reciprocal(out=sx[:, sxo + 32:sxo + 33], in_=rt1[:, 260:261])
                    nc.vector.reciprocal(out=sx[:, sxo + 33:sxo + 34], in_=rt1[:, 392:393])
                    nc.vector.tensor_tensor(out=sx[:, sxo + 34:sxo + 35], in0=sx[:, sxo + 32:sxo + 33],
                                            in1=am_t[:, acol + qs:acol + qs + 1], op=ALU.mult)
                    nc.vector.tensor_tensor(out=sx[:, sxo + 35:sxo + 36], in0=sx[:, sxo + 33:sxo + 34],
                                            in1=al_t[:, acol + qs:acol + qs + 1], op=ALU.mult)
                    nc.vector.tensor_scalar(out=rt2[:, 0:128], in0=rt1[:, 132:260],
                                            scalar1=sx[:, sxo + 34:sxo + 35],
                                            scalar2=0.0, op0=ALU.mult, op1=ALU.add)
                    slot = par * 2 + qs
                    need("ve", f"attsl{slot}")
                    VOS[slot] = inc(nc.vector.scalar_tensor_tensor(
                        out=att[:, slot * 128:(slot + 1) * 128], in0=rt1[:, 264:392],
                        scalar=sx[:, sxo + 35:sxo + 36], in1=rt2[:, 0:128], op0=ALU.mult, op1=ALU.add), "v")
                if par == 1:
                    need("te", "bank7")
                    tT = None
                    for sl in range(4):
                        wait("te", "v", VOS[sl])
                        tT = nc.tensor.transpose(out=ps[:, _bk(7, sl * 128, 128)],
                                                 in_=att[:, sl * 128:(sl + 1) * 128],
                                                 identity=eye[:])
                    tTg = inc(tT, "t")
                    for sl in range(4):
                        done(f"attsl{sl}", "t", tTg)
                    dst = qo3h[:, h - 1:h + 1, m * 256:(m + 1) * 256]
                    if (h // 2) % 2 == 0:
                        wait("sc", "t", tTg)
                        c = inc(nc.scalar.copy(out=dst, in_=ps[:, _bk(7, 0, 512)]), "s")
                        done("bank7", "s", c)
                    else:
                        wait("ve", "t", tTg)
                        c = inc(nc.vector.tensor_copy(out=dst, in_=ps[:, _bk(7, 0, 512)]), "v")
                        done("bank7", "v", c)
        te_barrier()

        for r in range(3):
            gg = GUARD.get(f"wq3_{r}")
            if gg:
                wait("gp", gg[0], gg[1])
        g_war = inc(nc.gpsimd.dma_start(out=wtmp[:, 0:NDC * 136],
                                        in_=dwar.rearrange("p c k -> p (c k)")), "wl")
        g_fm_ready = 0
        for tt in range(TILE // 128):
            hb = tt % 2
            sxo = (tt % 4) * 64
            need("te", f"bank{hb}")
            wait("te", "wl", g_war)
            for d in range(NDC):
                mm = nc.tensor.matmul(out=ps[:, _bk(hb, 0, 136)],
                                      lhsT=xn_fm[:, d * TILE + tt * 128: d * TILE + (tt + 1) * 128],
                                      rhs=wtmp[:, d * 136:(d + 1) * 136],
                                      start=(d == 0), stop=(d == NDC - 1))
            tH = inc(mm, "t")
            wait("ve", "t", tH)
            wait("ve", "dc", c_consts)
            sbase = 256 + (tt % 2) * 128
            SC8 = lambda o: sx[:, sbase + o: sbase + o + 8]
            SC1 = lambda o: sx[:, sbase + o: sbase + o + 1]
            z0 = SC8(0); zs = SC8(8); zc = SC8(16)
            nc.vector.tensor_tensor(out=z0, in0=ps[:, _bk(hb, 128, 8)], in1=br_t[:], op=ALU.add)
            VF()
            nc.vector.tensor_copy(out=zs, in_=z0)
            VF()
            ces = [(0, 1), (2, 3), (4, 5), (6, 7), (0, 2), (1, 3), (4, 6), (5, 7),
                   (1, 2), (5, 6), (0, 4), (1, 5), (2, 6), (3, 7), (2, 4), (3, 5),
                   (1, 2), (3, 4), (5, 6)]
            for (i, j) in ces:
                nc.vector.tensor_tensor(out=SC1(84), in0=zs[:, i:i + 1],
                                        in1=zs[:, j:j + 1], op=ALU.max)
                nc.vector.tensor_tensor(out=zs[:, j:j + 1], in0=zs[:, i:i + 1],
                                        in1=zs[:, j:j + 1], op=ALU.min)
                nc.vector.tensor_copy(out=zs[:, i:i + 1], in_=SC1(84))
            nc.vector.tensor_tensor(out=SC1(72), in0=zs[:, 0:1], in1=zs[:, 1:2], op=ALU.add)
            nc.vector.tensor_tensor(out=SC1(73), in0=zs[:, 2:3], in1=zs[:, 3:4], op=ALU.add)
            nc.vector.tensor_tensor(out=SC1(74), in0=zs[:, 4:5], in1=zs[:, 5:6], op=ALU.add)
            nc.vector.tensor_tensor(out=SC1(75), in0=zs[:, 6:7], in1=zs[:, 7:8], op=ALU.add)
            nc.vector.tensor_copy(out=zc[:, 0:1], in_=zs[:, 0:1])
            nc.vector.tensor_tensor(out=SC1(76), in0=SC1(72), in1=SC1(73), op=ALU.add)
            nc.vector.tensor_copy(out=zc[:, 1:2], in_=SC1(72))
            nc.vector.tensor_tensor(out=zc[:, 2:3], in0=SC1(72), in1=zs[:, 2:3], op=ALU.add)
            nc.vector.tensor_copy(out=zc[:, 3:4], in_=SC1(76))
            nc.vector.tensor_tensor(out=SC1(77), in0=SC1(76), in1=SC1(74), op=ALU.add)
            nc.vector.tensor_tensor(out=zc[:, 4:5], in0=SC1(76), in1=zs[:, 4:5], op=ALU.add)
            nc.vector.tensor_copy(out=zc[:, 5:6], in_=SC1(77))
            nc.vector.tensor_tensor(out=zc[:, 6:7], in0=SC1(77), in1=zs[:, 6:7], op=ALU.add)
            nc.vector.tensor_tensor(out=zc[:, 7:8], in0=SC1(77), in1=SC1(75), op=ALU.add)
            nc.vector.tensor_tensor(out=SC8(24), in0=zs, in1=kv_t[:], op=ALU.mult)
            VF()
            nc.vector.tensor_scalar(out=SC8(32), in0=SC8(24), scalar1=1.0, scalar2=0.0,
                                    op0=ALU.add, op1=ALU.add)
            VF()
            sup = SC8(40)
            nc.vector.tensor_tensor(out=sup, in0=SC8(32), in1=zc, op=ALU.is_gt)
            VF()
            nc.vector.tensor_reduce(out=SC1(80), in_=sup, axis=AXX, op=ALU.add)
            dsel = SC8(48)
            nc.vector.tensor_copy(out=dsel[:, 7:8], in_=sup[:, 7:8])
            nc.vector.tensor_tensor(out=dsel[:, 0:7], in0=sup[:, 0:7], in1=sup[:, 1:8],
                                    op=ALU.subtract)
            VF()
            nc.vector.tensor_tensor(out=SC8(56), in0=zc, in1=dsel, op=ALU.mult)
            nc.vector.reciprocal(out=SC1(83), in_=SC1(80))
            nc.vector.tensor_reduce(out=SC1(81), in_=SC8(56), axis=AXX, op=ALU.add)
            VF()
            nc.vector.tensor_scalar(out=SC1(82), in0=SC1(81), scalar1=1.0, scalar2=0.0,
                                    op0=ALU.subtract, op1=ALU.add)
            VF()
            nc.vector.tensor_tensor(out=SC1(85), in0=SC1(82), in1=SC1(83), op=ALU.mult)
            VF()
            w8 = SC8(64)
            nc.vector.tensor_scalar(out=w8, in0=z0, scalar1=SC1(85), scalar2=0.0,
                                    op0=ALU.subtract, op1=ALU.max)
            VF()
            need("ve", "gpre")
            for e in range(8):
                mm = nc.vector.tensor_scalar(out=gpre[:, e * 16:(e + 1) * 16],
                                             in0=ps[:, _bk(hb, e * 16, 16)],
                                             scalar1=w8[:, e:e + 1], scalar2=0.0,
                                             op0=ALU.mult, op1=ALU.add)
            vG = inc(VF(), "v")
            done(f"bank{hb}", "v", vG)
            gb = 2 + tt % 2
            need("te", f"bank{gb}")
            wait("te", "v", vG)
            tG = inc(nc.tensor.transpose(out=ps[:, _bk(gb, 0, 128)], in_=gpre[:],
                                         identity=eye[:]), "t")
            done("gpre", "t", tG)
            need("sc", "g_fm")
            wait("sc", "t", tG)
            sG = inc(nc.scalar.copy(out=g_fm[:, tt * 128:(tt + 1) * 128],
                                    in_=ps[:, _bk(gb, 0, 128)]), "s")
            done(f"bank{gb}", "s", sG)
            g_fm_ready = sG
            GUARD["war_used"] = ("t", tH)
        te_barrier()

        need("gp", "war_used")
        g_lb = inc(nc.gpsimd.dma_start(out=wtmp[:, 0:4096], in_=dlb), "wl")
        need("sy", "xbuf0")
        rows0 = tau * TILE
        def wob_ap(nm, reg):
            if reg < 2:
                buf = wob_o if nm == "o" else wob_b
                return buf[:, reg * 512:(reg + 1) * 512]
            return wtmp[:, 4608:5120] if nm == "o" else wtmp[:, 5120:5632]

        def wob_load(step):
            oc_, d_ = divmod(step, NDC)
            reg = step % 3
            if reg == 2:
                need("gp", "wq3_2")
                need("gp", "war_used")
            need("gp", f"wobo{reg}")
            c = inc(nc.gpsimd.dma_start(out=wob_ap("o", reg),
                                        in_=dwo[oc_, :, d_, :]), f"wo{reg}")
            GUARD[f"wobldo{reg}"] = (f"wo{reg}", c)
            need("gp", f"wobb{reg}")
            c = inc(nc.gpsimd.dma_start(out=wob_ap("b", reg),
                                        in_=dwb[oc_, :, d_, :]), f"wb{reg}")
            GUARD[f"wobldb{reg}"] = (f"wb{reg}", c)

        wob_load(0)
        wob_load(1)
        wob_load(2)
        for oc in range(8):
            need("gp", "bbt")
            g_bb = inc(nc.gpsimd.dma_start(out=bbt[:], in_=dbb[:, oc * 512:(oc + 1) * 512]), "bb")
            for tt in range(2):
                need("sy", f"xres{tt}")
                c = inc(nc.sync.dma_start(
                    out=xres[:, tt * 512:(tt + 1) * 512],
                    in_=dx[HALO + rows0 + tt * 128: HALO + rows0 + (tt + 1) * 128,
                           oc * 512:(oc + 1) * 512]), f"xr{tt}")
                GUARD[f"xresld{tt}"] = (f"xr{tt}", c)
            for d in range(NDC):
                step = oc * NDC + d
                reg = step % 3
                for nm in ("o", "b"):
                    gg = GUARD[f"wobld{nm}{reg}"]
                    wait("te", gg[0], gg[1])
                mm = None
                for tt in range(8):
                    if d == 0:
                        need("te", f"bank{tt}")
                    nc.tensor.matmul(
                        out=ps[:, _bk(tt)],
                        lhsT=qo_fm[:, d * TILE + tt * 128: d * TILE + (tt + 1) * 128],
                        rhs=wob_ap("o", reg),
                        start=(d == 0), stop=False)
                for tt in range(8):
                    mm = nc.tensor.matmul(
                        out=ps[:, _bk(tt)],
                        lhsT=xn_fm[:, d * TILE + tt * 128: d * TILE + (tt + 1) * 128],
                        rhs=wob_ap("b", reg),
                        start=False, stop=False)
                tC = inc(mm, "t")
                done(f"wobo{reg}", "t", tC)
                done(f"wobb{reg}", "t", tC)
                if step + 3 < 8 * NDC:
                    wob_load(step + 3)
            wait("te", "wl", g_lb)
            wait("te", "bb", g_bb)
            wait("te", "s", g_fm_ready)
            wait("te", "dc", g_on)
            for tt in range(8):
                nc.tensor.matmul(out=ps[:, _bk(tt)],
                                 lhsT=g_fm[:, tt * 128:(tt + 1) * 128],
                                 rhs=wtmp[:, oc * 512:(oc + 1) * 512],
                                 start=False, stop=False)
            tlast = {}
            for tt in range(8):
                mm = nc.tensor.matmul(out=ps[:, _bk(tt)], lhsT=ones16[:],
                                      rhs=bbt[:], start=False, stop=True)
                tlast[tt] = inc(mm, "t")
            done("bbt", "t", tlast[7])
            for tt in range(8):
                r2 = tt % 2
                wait("ve", "t", tlast[tt])
                gg = GUARD[f"xresld{r2}"]
                wait("ve", gg[0], gg[1])
                need("ve", f"outst{r2}")
                vE = inc(nc.vector.tensor_tensor(out=outst[:, r2 * 512:(r2 + 1) * 512],
                                                 in0=ps[:, _bk(tt)],
                                                 in1=xres[:, r2 * 512:(r2 + 1) * 512],
                                                 op=ALU.add), "v")
                done(f"bank{tt}", "v", vE)
                done(f"xres{r2}", "v", vE)
                wait("sy", "v", vE)
                cS = inc(nc.sync.dma_start(
                    out=dout[rows0 + tt * 128: rows0 + (tt + 1) * 128, oc * 512:(oc + 1) * 512],
                    in_=outst[:, r2 * 512:(r2 + 1) * 512]), f"so{r2}")
                done(f"outst{r2}", f"so{r2}", cS)
                if tt + 2 < 8:
                    need("sy", f"xres{r2}")
                    c = inc(nc.sync.dma_start(
                        out=xres[:, r2 * 512:(r2 + 1) * 512],
                        in_=dx[HALO + rows0 + (tt + 2) * 128: HALO + rows0 + (tt + 3) * 128,
                               oc * 512:(oc + 1) * 512]), f"xr{r2}")
                    GUARD[f"xresld{r2}"] = (f"xr{r2}", c)
        te_barrier()
        t_all = CNT["t"]
        for r in range(3):
            done(f"wq3_{r}", "t", t_all)
        done("wtmp_lb", "t", t_all)
        done("g_fm", "t", t_all)
        done("qo_free", "t", t_all)

    nc.sync.wait_ge(sem["so0"], CNT["so0"])
    nc.sync.wait_ge(sem["so1"], CNT["so1"])
    fin_t = inc(nc.tensor.nop(), "t")
    fin_v = inc(nc.vector.nop(), "v")
    fin_s = inc(nc.scalar.nop(), "s")
    fin_q = inc(nc.sync.nop(), "s", 1)
    nc.gpsimd.wait_ge(sem["t"], fin_t)
    nc.gpsimd.wait_ge(sem["v"], fin_v)
    nc.gpsimd.wait_ge(sem["s"], fin_q)
    for k in DMAK:
        if CNT[k]:
            nc.gpsimd.wait_ge(sem[k], CNT[k])
    for k in semkeys:
        nc.gpsimd.sem_clear(sem[k])
    ctx.close()
    return nc



def _host_prep(inputs):
    x = np.asarray(inputs["L0_attn_x"], dtype=np.float32).reshape(S, D)
    Wq = np.asarray(inputs["L0_attn_Wq"], dtype=np.float32)
    Wk = np.asarray(inputs["L0_attn_Wk"], dtype=np.float32)
    Wv = np.asarray(inputs["L0_attn_Wv"], dtype=np.float32)
    Wo = np.asarray(inputs["L0_attn_Wo"], dtype=np.float32)
    Wb = np.asarray(inputs["L0_ffn_W_base"], dtype=np.float32)
    bb = np.asarray(inputs["L0_ffn_b_base"], dtype=np.float32)
    Wr = np.asarray(inputs["L0_ffn_W_router"], dtype=np.float32)
    br = np.asarray(inputs["L0_ffn_b_router"], dtype=np.float32)
    lA = np.asarray(inputs["L0_ffn_lora_A"], dtype=np.float32)
    lB = np.asarray(inputs["L0_ffn_lora_B"], dtype=np.float32)

    wq_arr = np.ascontiguousarray(Wq.reshape(H, DH, NDC, 128).transpose(0, 3, 2, 1))
    wo_arr = np.ascontiguousarray(Wo.reshape(8, 512, NDC, 128).transpose(0, 3, 2, 1))
    wb_arr = np.ascontiguousarray(Wb.reshape(8, 512, NDC, 128).transpose(0, 3, 2, 1))
    wk_arr = np.ascontiguousarray(Wk.reshape(128, NDC, 128).transpose(2, 1, 0))
    wv_arr = np.ascontiguousarray(Wv.reshape(128, NDC, 128).transpose(2, 1, 0))
    A_t = lA.transpose(1, 0, 2).reshape(D, 128)
    war = np.ascontiguousarray(np.concatenate([A_t, Wr.T], axis=1)
                               .reshape(NDC, 128, 136).transpose(1, 0, 2))
    lb2 = np.ascontiguousarray(lB.reshape(128, D))
    bb16 = np.zeros((16, D), np.float32)
    bb16[0] = bb
    br_rep = np.ascontiguousarray(np.broadcast_to(br[None, :], (128, 8)))
    kvec = np.ascontiguousarray(np.broadcast_to(np.arange(1, 9, dtype=np.float32)[None, :],
                                                (128, 8)))
    eye = np.eye(128, dtype=np.float32)
    eyb = np.eye(128, dtype=np.float32).astype(ml_dtypes.bfloat16)
    rsw = np.zeros((128, 128), np.float32)
    for i in range(128):
        rsw[i, (i + 64) % 128] = 1.0
    rsw = rsw.astype(ml_dtypes.bfloat16)
    on16 = np.zeros((16, 128), np.float32)
    on16[0, :] = 1.0
    on16 = on16.astype(ml_dtypes.bfloat16)

    inv_freq = 1.0 / 10000.0 ** (np.arange(0, DH, 2, dtype=np.float32) / DH)
    t = np.arange(S, dtype=np.float32)
    f = np.outer(t, inv_freq)
    cosf = np.zeros((S, DH), dtype=np.float32)
    sinf = np.zeros((S, DH), dtype=np.float32)
    cosf[:, 0::2] = np.cos(f); cosf[:, 1::2] = np.cos(f)
    sinf[:, 0::2] = np.sin(f); sinf[:, 1::2] = np.sin(f)
    cosT = np.ascontiguousarray(cosf.T)
    sinT = sinf.T.copy()
    sinT[0:64, :] *= -1.0

    alpha = np.linspace(0.0, 1.0, 256, dtype=np.float32)

    in_maps = []
    for c in range(NCORES):
        lo = c * TC - HALO
        xp = np.zeros((PAD, D), np.float32)
        s0, s1 = max(lo, 0), min(lo + PAD, S)
        xp[s0 - lo: s1 - lo] = x[s0:s1]
        cs_c = np.zeros((128, PAD), np.float32)
        sn_c = np.zeros((128, PAD), np.float32)
        cs_c[:, s0 - lo: s1 - lo] = cosT[:, s0:s1]
        sn_c[:, s0 - lo: s1 - lo] = sinT[:, s0:s1]
        al_t = np.empty((128, 16), np.float32)
        am_t = np.empty((128, 16), np.float32)
        for b in range(8):
            for qs in range(2):
                col = 2 * b + qs
                a = alpha[qs * 128:(qs + 1) * 128]
                if c == 0 and b == 0:
                    al_t[:, col] = 1.0
                    am_t[:, col] = 0.0
                else:
                    al_t[:, col] = a
                    am_t[:, col] = 1.0 - a
        vc_t = np.zeros((128, KT), np.float32)
        for kt in range(KT):
            g = lo + kt * 128 + np.arange(128)
            vc_t[:, kt] = ((g >= 0) & (g < S)).astype(np.float32)
        in_maps.append({
            "xp": xp, "wq": wq_arr, "wo": wo_arr, "wb": wb_arr, "wk": wk_arr,
            "wv": wv_arr, "war": war, "lb": lb2, "bb": bb16, "cs": cs_c, "sn": sn_c,
            "al": al_t, "am": am_t, "vc": vc_t, "br": br_rep, "kv": kvec,
            "ey": eye, "eyb": eyb, "rs": rsw, "on16": on16,
        })
    return in_maps


def kernel(**inputs):
    in_maps = _host_prep(inputs)
    nc = build_nc()
    res = run_bass_kernel_spmd(nc, in_maps, core_ids=list(range(NCORES)))
    out = np.concatenate([res.results[c]["out"] for c in range(NCORES)], axis=0)
    return np.ascontiguousarray(out.reshape(1, S, D), dtype=np.float32)
